# version 04
# baseline (speedup 1.0000x reference)
import contextlib
import numpy as np
import ml_dtypes
import concourse.bass as bass
import concourse.mybir as mybir
from concourse.bass_utils import run_bass_kernel_spmd

BF16 = mybir.dt.bfloat16
F32 = mybir.dt.float32
AF = mybir.ActivationFunctionType
OP = mybir.AluOpType
AX = mybir.AxisListType
bfdt = ml_dtypes.bfloat16

NC = 8
T = 2048
D = 2048
HD = 8192
NTOK = 16384
QSCALE = 1.0 / float(np.sqrt(2048.0))
FWHT_SCALE = 0.5 / float(np.sqrt(8192.0))
EPS = 1e-5
GQ = 2

ENGINES = ("sync", "gpsimd", "vector", "scalar", "tensor")


class P:

    def __init__(self, nc):
        self.nc = nc
        self.ops = []
        self.ct = {}
        self.last_wait = {}

    def sem(self, name):
        h = self.nc.alloc_semaphore(name=name)
        self.ct[h] = 0
        return h

    def inc(self, h, n=1):
        self.ct[h] += n
        return self.ct[h]

    def on(self, eng, fn):
        self.ops.append((eng, fn))

    def dma(self, qeng, sem, out, in_):
        tgt = self.inc(sem, 16)
        self.on(qeng, lambda e, o=out, i=in_, s=sem: e.dma_start(out=o, in_=i).then_inc(s, 16))
        return tgt

    def wait(self, eng, sem, tgt):
        if tgt <= 0:
            return
        key = (eng, sem)
        if self.last_wait.get(key, -1) >= tgt:
            return
        self.last_wait[key] = tgt
        self.on(eng, lambda e, s=sem, t=tgt: e.wait_ge(s, t))

    def barrier(self, bar):
        tgt = self.ct[bar] + 5
        for en in ENGINES:
            self.on(en, lambda e, b=bar: e.sem_inc(b, 1))
        self.ct[bar] = tgt
        for en in ENGINES:
            self.wait(en, bar, tgt)

    def chain(self, eng, sem, fn):
        t = self.ct[sem]
        def g(e, fn=fn, s=sem, t=t):
            if t:
                e.wait_ge(s, t)
            fn(e).then_inc(s, 1)
        self.on(eng, g)
        return self.inc(sem)

    def mm(self, out, lhsT, rhs, start, stop, sem=None):
        tgt = self.inc(sem) if sem is not None else None
        def f(e, o=out, l=lhsT, r=rhs, st=start, sp=stop, s=sem):
            ins = e.matmul(o, l, r, start=st, stop=sp)
            if s is not None:
                ins.then_inc(s, 1)
        self.on("tensor", f)
        return tgt

    def emit(self):
        with self.nc.Block() as block:
            for name in ENGINES:
                def run(e, name=name):
                    for n, fn in self.ops:
                        if n == name:
                            fn(e)
                getattr(block, name)(run)


def tr_batch(cm, srcs, evac_fn, banks=(0, 1)):
    p = cm.p
    ev = []
    ngroups = (len(srcs) + 3) // 4
    for gi in range(ngroups):
        bank = banks[gi % 2]
        if gi >= 2:
            p.wait("tensor", cm.dv, ev[gi - 2])
        chunk = srcs[gi * 4:(gi + 1) * 4]
        for j, s in enumerate(chunk):
            p.mm(cm.pbk[bank][:, j * 128:(j + 1) * 128], s, cm.ident[:, :],
                 True, True, sem=(cm.pe if j == len(chunk) - 1 else None))
        p.wait("vector", cm.pe, p.ct[cm.pe])
        ev.append(evac_fn(gi, cm.pbk[bank][:, 0:len(chunk) * 128]))
    return ev[-1]


def sb(nc, st, name, shape, dtype):
    return st.enter_context(nc.sbuf_tensor(name, shape, dtype))


def ps(nc, st, name, shape, dtype=F32):
    return st.enter_context(nc.psum_tensor(name, shape, dtype))



def build_l0():
    nc = bass.Bass("TRN2", target_bir_lowering=False, detect_race_conditions=False)
    st = contextlib.ExitStack()
    w1s = nc.dram_tensor("w1s", [2 * D, 512], F32, kind="ExternalInput").ap()
    w3s = nc.dram_tensor("w3s", [2 * D, 512], F32, kind="ExternalInput").ap()
    w2s = nc.dram_tensor("w2s", [256, HD], F32, kind="ExternalInput").ap()
    idin = nc.dram_tensor("idin", [128, 128], BF16, kind="ExternalInput").ap()
    w1b = nc.dram_tensor("w1b", [2 * D, 512], BF16, kind="ExternalOutput").ap()
    w3b = nc.dram_tensor("w3b", [2 * D, 512], BF16, kind="ExternalOutput").ap()
    w2t = nc.dram_tensor("w2t", [HD, 256], BF16, kind="ExternalOutput").ap()

    p = P(nc)
    ld, gs, pe, dv, bar = (p.sem(n) for n in ("ld", "gs", "pe", "dv", "bar"))

    cast = sb(nc, st, "cast", [128, 4096], BF16)
    fa = sb(nc, st, "fa", [128, HD], F32)
    fb = sb(nc, st, "fb", [128, HD], F32)
    fbf = sb(nc, st, "fbf", [128, HD], BF16)
    ident = sb(nc, st, "ident", [128, 128], BF16)
    tst = sb(nc, st, "tst", [128, 64 * 128], BF16)
    pt = [ps(nc, st, f"pt{i}", [128, 512], F32) for i in range(2)]

    idt = p.dma("sync", ld, ident[:, :], idin)

    for src, dst in ((w1s, w1b), (w3s, w3b)):
        for blk in range(4):
            in_ap = bass.AP(src.tensor, blk * 1024 * 512, [[512, 128], [128 * 512, 8], [1, 512]])
            p.wait("gpsimd", gs, p.ct[gs])
            t = p.dma("gpsimd", gs, cast[:, :], in_ap)
            p.wait("gpsimd", gs, t)
            out_ap = bass.AP(dst.tensor, blk * 1024 * 512, [[512, 128], [128 * 512, 8], [1, 512]])
            p.dma("gpsimd", gs, out_ap, cast[:, :])

    for blk in range(2):
        if blk == 1:
            p.wait("sync", dv, p.ct[dv])
            p.wait("vector", gs, p.ct[gs])
        t = p.dma("sync", ld, fa[:, :], w2s[blk * 128:(blk + 1) * 128, :])
        p.wait("vector", ld, t)
        src, dst = fa, fb
        h = 1
        while h < HD:
            nblk = HD // (2 * h)
            a_in = bass.AP(src, 0, [[HD, 128], [2 * h, nblk], [1, h]])
            b_in = bass.AP(src, h, [[HD, 128], [2 * h, nblk], [1, h]])
            a_out = bass.AP(dst, 0, [[HD, 128], [2 * h, nblk], [1, h]])
            b_out = bass.AP(dst, h, [[HD, 128], [2 * h, nblk], [1, h]])
            p.on("vector", lambda e, o=a_out, x=a_in, y=b_in: e.tensor_add(o, x, y))
            p.on("vector", lambda e, o=b_out, x=a_in, y=b_in:
                 e.tensor_sub(o, x, y).then_inc(dv, 1))
            p.wait("vector", dv, p.inc(dv))
            src, dst = dst, src
            h *= 2
        def scl(e, s=src):
            e.tensor_scalar_mul(fbf[:, :], s[:, :], FWHT_SCALE).then_inc(dv, 1)
        p.on("vector", scl)
        dvt = p.inc(dv)
        p.wait("tensor", dv, dvt)
        p.wait("tensor", ld, p.ct[ld])
        evt = []
        for hc in range(64):
            bank = hc % 2
            if hc >= 2:
                p.wait("tensor", dv, evt[hc - 2])
            p.mm(pt[bank][:, 0:128],
                 fbf[:, hc * 128:(hc + 1) * 128], ident[:, :], True, True, sem=pe)
            p.wait("vector", pe, p.ct[pe])
            p.on("vector", lambda e, hc=hc, bank=bank:
                 e.tensor_copy(tst[:, hc * 128:(hc + 1) * 128],
                               pt[bank][:, 0:128]).then_inc(dv, 1))
            evt.append(p.inc(dv))
        p.wait("gpsimd", dv, p.ct[dv])
        out_ap = bass.AP(w2t.tensor, blk * 128, [[256, 128], [128 * 256, 64], [1, 128]])
        p.dma("gpsimd", gs, out_ap, tst[:, :])
        p.barrier(bar)

    p.wait("gpsimd", gs, p.ct[gs])
    p.wait("sync", ld, p.ct[ld])
    p.emit()
    st.close()
    return nc



class CM:
    def __init__(self, nc, st, p):
        self.nc, self.p, self.st = nc, p, st
        self.pa = sb(nc, st, "pa", [128, 32768], BF16)
        self.pb = sb(nc, st, "pb", [128, 24576], BF16)
        self.xt = [sb(nc, st, f"xt{i}", [128, 2048], F32) for i in range(2)]
        self.sq = sb(nc, st, "sq", [128, 2048], F32)
        self.fstage = [sb(nc, st, f"fstage{i}", [128, 2048], F32) for i in range(2)]
        self.xres = [sb(nc, st, f"xres{i}", [128, 512], F32) for i in range(2)]
        self.xn = sb(nc, st, "xn", [128, 2048], BF16)
        self.zst = [sb(nc, st, f"zst{i}", [128, 2048], BF16) for i in range(2)]
        self.bst = [sb(nc, st, f"bst{i}", [128, 2048], BF16) for i in range(2)]
        self.s1 = sb(nc, st, "s1", [128, 1], F32)
        self.s2 = sb(nc, st, "s2", [128, 1], F32)
        self.s3 = sb(nc, st, "s3", [128, 1], F32)
        self.s4 = sb(nc, st, "s4", [128, 1], F32)
        self.nm = [sb(nc, st, f"nm{i}", [128, 32], F32) for i in range(GQ)]
        self.rs = [sb(nc, st, f"rs{i}", [128, 32], F32) for i in range(GQ)]
        self.fc = [sb(nc, st, f"fc{i}", [128, 32], F32) for i in range(GQ)]
        self.den = [sb(nc, st, f"den{i}", [128, 1], F32) for i in range(GQ)]
        self.rden = [sb(nc, st, f"rden{i}", [128, 1], F32) for i in range(GQ)]
        self.ident = sb(nc, st, "ident", [128, 128], BF16)
        self.ones = sb(nc, st, "ones", [1, 512], BF16)
        self.bias = {}
        self.pbk = [ps(nc, st, f"pbk{i}", [128, 512], F32) for i in range(8)]
        self.ld = p.sem("ld")
        self.gq = p.sem("gq")
        self.cs = p.sem("cs")
        self.zl = p.sem("zl")
        self.sl = [p.sem(f"sl{i}") for i in range(8)]
        self.ssm = [p.sem(f"ss{i}") for i in range(6)]
        self.pe = p.sem("pe")
        self.dv = p.sem("dv")
        self.ac = p.sem("ac")
        self.bar = p.sem("bar")

    def final_waits(self):
        p = self.p
        for s in [self.ld, self.gq, self.cs, self.zl] + self.sl + self.ssm:
            if p.ct[s]:
                p.wait("gpsimd", s, p.ct[s])

    def barrier(self):
        p = self.p
        for s in self.ssm + [self.gq]:
            if p.ct[s]:
                p.wait("gpsimd", s, p.ct[s])
        p.barrier(self.bar)

    def load_consts(self, idin, onein, bias_rows):
        p = self.p
        p.dma("sync", self.cs, self.ident[:, :], idin)
        p.dma("sync", self.cs, self.ones[:1, :], onein)
        for nm, ap in bias_rows.items():
            t = sb(self.nc, self.st, f"bias_{nm}", [1, 2048], BF16)
            self.bias[nm] = t
            p.dma("sync", self.cs, t[:1, :], ap)
        tgt = p.ct[self.cs]
        for en in ("tensor", "vector", "scalar", "sync"):
            p.wait(en, self.cs, tgt)


def emit_ln(cm, x_src, z_dram, src_is_f32):
    p = cm.p
    ld, gq, pe, dv, ac = cm.ld, cm.gq, cm.pe, cm.dv, cm.ac
    store_tgt = [0, 0]
    prev_var = 0
    prev_tp_done = 0
    prev_tr = 0
    xt_guard = [(0, 0), (0, 0)]
    for t in range(16):
        xt = cm.xt[t % 2] if src_is_f32 else cm.bst[t % 2]
        xsem = cm.sl[t % 2]
        gdv, gac = xt_guard[t % 2]
        p.wait("sync", dv, gdv)
        p.wait("sync", ac, gac)
        lt = p.dma("sync", xsem, xt[:, :], x_src[t * 128:(t + 1) * 128, :])
        p.wait("vector", xsem, lt)
        p.wait("scalar", xsem, lt)
        p.chain("vector", dv, lambda e, xt=xt: e.reduce_sum(cm.s1[:, :], xt[:, :],
                                                            axis=AX.X, negate=True))
        p.chain("vector", dv, lambda e: e.tensor_scalar_mul(cm.s1[:, :], cm.s1[:, :], 1.0 / 2048.0))
        p.wait("scalar", dv, prev_var)
        act = p.chain("scalar", ac, lambda e, xt=xt: e.activation(
            cm.sq[:, :], xt[:, :], AF.Square, accum_out=cm.s2[:, :]))
        p.wait("vector", ac, act)
        p.chain("vector", dv, lambda e: e.tensor_mul(cm.s3[:, :], cm.s1[:, :], cm.s1[:, :]))
        p.chain("vector", dv, lambda e: e.tensor_scalar(cm.s2[:, :], cm.s2[:, :], 1.0 / 2048.0,
                                                        EPS, op0=OP.mult, op1=OP.add))
        prev_var = p.chain("vector", dv, lambda e: e.tensor_sub(cm.s2[:, :], cm.s2[:, :],
                                                                cm.s3[:, :]))
        p.wait("scalar", dv, prev_var)
        p.chain("scalar", ac, lambda e: e.activation(cm.s3[:, :], cm.s2[:, :], AF.Ln))
        act2 = p.chain("scalar", ac, lambda e: e.activation(cm.s4[:, :], cm.s3[:, :], AF.Exp,
                                                            scale=-0.5))
        p.wait("vector", ac, act2)
        p.wait("vector", pe, prev_tp_done)
        dvt2 = p.chain("vector", dv, lambda e, xt=xt: e.tensor_scalar(
            cm.xn[:, :], xt[:, :], cm.s1[:, :], cm.s4[:, :], op0=OP.add, op1=OP.mult))
        xt_guard[t % 2] = (dvt2, act)
        zstg = cm.zst[t % 2]
        p.wait("tensor", dv, dvt2)
        p.wait("tensor", dv, prev_tr)
        if t >= 2:
            p.wait("vector", cm.ssm[2 + t % 2], p.ct[cm.ssm[2 + t % 2]])

        def ev_ln(gi, bank_ap, zstg=zstg):
            p.on("vector", lambda e, gi=gi, b=bank_ap, zstg=zstg:
                 e.tensor_copy(zstg[:, gi * 512:(gi + 1) * 512], b).then_inc(dv, 1))
            return p.inc(dv)
        prev_tr = tr_batch(cm, [cm.xn[:, dc * 128:(dc + 1) * 128] for dc in range(16)], ev_ln)
        prev_tp_done = p.ct[pe]
        p.wait("gpsimd", dv, p.ct[dv])
        out_ap = bass.AP(z_dram.tensor, t * 128, [[2048, 128], [128 * 2048, 16], [1, 128]])
        store_tgt[t % 2] = p.dma("gpsimd", cm.ssm[2 + t % 2], out_ap, zstg[:, :])
    cm.barrier()


def emit_projs(cm, z_dram, projs):
    p = cm.p
    ld, gq, pe, dv, ac = cm.ld, cm.gq, cm.pe, cm.dv, cm.ac
    for ci in range(16):
        p.dma("sync", cm.zl, cm.pa[:, ci * 2048:(ci + 1) * 2048],
              z_dram[ci * 128:(ci + 1) * 128, :])
    p.wait("tensor", cm.zl, p.ct[cm.zl])
    evac_done = {}
    store_done = {}
    mm_fin = {}
    prev_tr = 0
    for pi, (w_dram, bias_name, out_dram, scale, v_transpose) in enumerate(projs):
        for c4 in range(4):
            j = pi * 4 + c4
            in_ap = bass.AP(w_dram.tensor, c4 * 512, [[2048, 128], [128 * 2048, 16], [1, 512]])
            wcol = cm.pb[:, (j % 2) * 8192:(j % 2) * 8192 + 8192]
            wsem = cm.sl[2 + j % 2]
            if j >= 2:
                p.wait("sync", pe, mm_fin[j - 2])
            wt = p.dma("sync", wsem, wcol, in_ap)
            p.wait("tensor", wsem, wt)
            for c in range(4):
                cc = c4 * 4 + c
                jj = pi * 16 + cc
                bs = (jj % 2) * 4
                if jj >= 2:
                    for t4 in range(4):
                        p.wait("tensor", ac, evac_done[(jj - 2, t4)])
                if v_transpose:
                    p.wait("tensor", dv, prev_tr)
                for ci in range(16):
                    for t4 in range(4):
                        p.mm(cm.pbk[bs + t4][:, :],
                             wcol[:, ci * 512 + c * 128: ci * 512 + (c + 1) * 128],
                             cm.pa[:, ci * 2048 + t4 * 512: ci * 2048 + (t4 + 1) * 512],
                             ci == 0, False)
                stg = cm.bst[jj % 2]
                for t4 in range(4):
                    pet = p.mm(cm.pbk[bs + t4][:, :],
                               cm.bias[bias_name][:1, cc * 128:(cc + 1) * 128],
                               cm.ones[:1, :512], False, True, sem=pe)
                    p.wait("scalar", pe, pet)
                    if t4 == 0 and jj >= 2:
                        p.wait("scalar", cm.ssm[jj % 2], p.ct[cm.ssm[jj % 2]])
                    p.on("scalar", lambda e, t4=t4, stg=stg, s=scale, bs=bs:
                         e.mul(stg[:, t4 * 512:(t4 + 1) * 512],
                               cm.pbk[bs + t4][:, :], s).then_inc(ac, 1))
                    evac_done[(jj, t4)] = p.inc(ac)
                if not v_transpose:
                    p.wait("gpsimd", ac, evac_done[(jj, 3)])
                    store_done[jj] = p.dma("gpsimd", cm.ssm[jj % 2],
                                           out_dram[cc * 128:(cc + 1) * 128, :], stg[:, :])
                else:
                    p.wait("tensor", ac, evac_done[(jj, 3)])
                    tstg = cm.zst[jj % 2]
                    p.wait("tensor", dv, prev_tr)
                    if p.ct[cm.ssm[2 + jj % 2]]:
                        p.wait("vector", cm.ssm[2 + jj % 2], p.ct[cm.ssm[2 + jj % 2]])

                    def ev_v(gi, bank_ap, tstg=tstg):
                        p.on("vector", lambda e, gi=gi, b=bank_ap, tstg=tstg:
                             e.tensor_copy(tstg[:, gi * 512:(gi + 1) * 512], b).then_inc(dv, 1))
                        return p.inc(dv)
                    prev_tr = tr_batch(cm, [stg[:, tt * 128:(tt + 1) * 128] for tt in range(16)],
                                       ev_v, banks=(bs + 2, bs + 3))
                    p.wait("gpsimd", dv, p.ct[dv])
                    out_ap = bass.AP(out_dram.tensor, cc * 128,
                                     [[2048, 128], [128 * 2048, 16], [1, 128]])
                    store_done[jj] = p.dma("gpsimd", cm.ssm[2 + jj % 2], out_ap, tstg[:, :])
            mm_fin[j] = p.ct[pe]
    cm.barrier()


def emit_attn_layer(cm, q_dram, kg_dram, vg_dram, wo_dram, bo_name, z_out_dram, dbg=None):
    p = cm.p
    ld, gq, pe, dv, ac = cm.ld, cm.gq, cm.pe, cm.dv, cm.ac
    NG = 16 // GQ
    womm = {}
    trc_pe = {}
    evac2_last = {}
    a5_last = {}
    bevac_last = {}
    for g in range(NG):
        q0 = g * GQ * 128
        qsb = cm.pb[:, 16384:16384 + 16 * GQ * 128]
        in_q = bass.AP(q_dram.tensor, q0, [[2048, 128], [128 * 2048, 16], [1, GQ * 128]])
        p.wait("sync", pe, trc_pe.get(g - 1, 0))
        qt = p.dma("sync", cm.zl, qsb, in_q)
        p.wait("tensor", cm.zl, qt)
        p.wait("tensor", dv, bevac_last.get(g - 1, 0))
        smm_done = {}
        exp_done = {}
        for kt in range(32):
            r, ltile = kt // 4, kt % 4
            kbuf = cm.pb[:, (kt % 2) * 8192:(kt % 2) * 8192 + 8192]
            if kt >= 2:
                p.wait("sync", pe, smm_done[(kt - 2, GQ - 1)])
                p.wait("gpsimd", pe, smm_done[(kt - 2, GQ - 1)])
            else:
                p.wait("sync", pe, womm.get((g - 1, 3), 0))
                p.wait("gpsimd", pe, womm.get((g - 1, 3), 0))
            in_k1 = bass.AP(kg_dram.tensor, (r * 2048) * 2048 + ltile * 512,
                            [[2048, 128], [128 * 2048, 8], [1, 512]])
            in_k2 = bass.AP(kg_dram.tensor, (r * 2048 + 8 * 128) * 2048 + ltile * 512,
                            [[2048, 128], [128 * 2048, 8], [1, 512]])
            ksem = cm.sl[4 + kt % 2]
            ksem2 = cm.sl[6 + kt % 2]
            ktg = p.dma("sync", ksem, kbuf[:, 0:4096], in_k1)
            ktg2 = p.dma("gpsimd", ksem2, kbuf[:, 4096:8192], in_k2)
            p.wait("tensor", ksem, ktg)
            p.wait("tensor", ksem2, ktg2)
            for qb in range(GQ):
                sbank = cm.pbk[qb]
                if kt >= 1:
                    p.wait("tensor", ac, exp_done[(kt - 1, qb)])
                for ci in range(16):
                    p.mm(sbank[:, :],
                         cm.pb[:, 16384 + ci * GQ * 128 + qb * 128:
                               16384 + ci * GQ * 128 + (qb + 1) * 128],
                         kbuf[:, ci * 512:(ci + 1) * 512],
                         ci == 0, ci == 15, sem=(pe if ci == 15 else None))
                smm_done[(kt, qb)] = p.ct[pe]
                p.wait("vector", pe, smm_done[(kt, qb)])
                p.chain("vector", dv, lambda e, sbank=sbank:
                        e.reduce_max(cm.s1[:, :], sbank[:, :], axis=AX.X, negate=True))
                if kt == 0:
                    dvm = p.chain("vector", dv, lambda e, qb=qb:
                                  e.tensor_copy(cm.nm[qb][:, 0:1], cm.s1[:, :]))
                else:
                    dvm = p.chain("vector", dv, lambda e, qb=qb, kt=kt:
                                  e.tensor_tensor(cm.nm[qb][:, kt:kt + 1],
                                                  cm.nm[qb][:, kt - 1:kt],
                                                  cm.s1[:, :], op=OP.min))
                p.wait("scalar", dv, dvm)
                p.on("scalar", lambda e, qb=qb, kt=kt, sbank=sbank:
                     e.activation(cm.pa[:, qb * 16384 + kt * 512: qb * 16384 + (kt + 1) * 512],
                                  sbank[:, :], AF.Exp, bias=cm.nm[qb][:, kt:kt + 1],
                                  accum_out=cm.rs[qb][:, kt:kt + 1]).then_inc(ac, 1))
                exp_done[(kt, qb)] = p.inc(ac)
        for qb in range(GQ):
            fct = p.chain("scalar", ac, lambda e, qb=qb: e.activation(
                cm.fc[qb][:, :], cm.nm[qb][:, :], AF.Exp, scale=-1.0,
                bias=cm.nm[qb][:, 31:32]))
            p.wait("vector", ac, fct)
            p.chain("vector", dv, lambda e, qb=qb: e.tensor_mul(cm.nm[qb][:, :],
                                                                cm.rs[qb][:, :], cm.fc[qb][:, :]))
            p.chain("vector", dv, lambda e, qb=qb: e.reduce_sum(cm.den[qb][:, :],
                                                                cm.nm[qb][:, :], axis=AX.X))
            rd_done = p.chain("vector", dv, lambda e, qb=qb:
                              e.reciprocal(cm.rden[qb][:, :], cm.den[qb][:, :]))
            for kt in range(32):
                def corr(e, qb=qb, kt=kt, last=(kt == 31)):
                    ins = e.tensor_scalar_mul(
                        cm.pa[:, qb * 16384 + kt * 512: qb * 16384 + (kt + 1) * 512],
                        cm.pa[:, qb * 16384 + kt * 512: qb * 16384 + (kt + 1) * 512],
                        cm.fc[qb][:, kt:kt + 1])
                    if last:
                        ins.then_inc(dv, 1)
                p.on("vector", corr)
            corr_done = p.inc(dv)
            if False and dbg is not None and g == 0:
                p.wait("gpsimd", dv, corr_done)
                p.dma("gpsimd", cm.gq, dbg["e"][:, qb * 16384:(qb + 1) * 16384],
                      cm.pa[:, qb * 16384:(qb + 1) * 16384])
                if qb == 0:
                    p.dma("gpsimd", cm.gq, dbg["nm"][:, :], cm.nm[0][:, :])
                    p.dma("gpsimd", cm.gq, dbg["rs"][:, :], cm.rs[0][:, :])
                    p.dma("gpsimd", cm.gq, dbg["rden"][:, :], cm.rden[0][:, :])
                p.wait("gpsimd", cm.gq, p.ct[cm.gq])
            p.wait("tensor", dv, corr_done)

            def ev_pt(gi, bank_ap, qb=qb):
                col = qb * 16384 + gi * 512
                p.on("vector", lambda e, col=col, b=bank_ap:
                     e.tensor_copy(cm.pa[:, col:col + 512], b).then_inc(dv, 1))
                return p.inc(dv)
            a5_last[g] = tr_batch(
                cm, [cm.pa[:, qb * 16384 + ch * 128: qb * 16384 + (ch + 1) * 128]
                     for ch in range(128)], ev_pt)
        p.wait("tensor", dv, a5_last[g])
        p.wait("tensor", ac, evac2_last.get(g - 1, 0))
        vld = {}
        pv_done = {}
        for kt in range(32):
            vbuf = cm.pb[:, (kt % 2) * 8192:(kt % 2) * 8192 + 8192]
            if kt >= 2:
                p.wait("sync", pe, pv_done[kt - 2])
                p.wait("gpsimd", pe, pv_done[kt - 2])
            else:
                p.wait("sync", pe, smm_done[(31, GQ - 1)])
                p.wait("gpsimd", pe, smm_done[(31, GQ - 1)])
            in_v1 = bass.AP(vg_dram.tensor, kt * 512 * 2048,
                            [[2048, 128], [128 * 2048, 2], [1, 2048]])
            in_v2 = bass.AP(vg_dram.tensor, (kt * 512 + 256) * 2048,
                            [[2048, 128], [128 * 2048, 2], [1, 2048]])
            vsem = cm.sl[4 + kt % 2]
            vsem2 = cm.sl[6 + kt % 2]
            vld[kt] = p.dma("sync", vsem, vbuf[:, 0:4096], in_v1)
            vld2 = p.dma("gpsimd", vsem2, vbuf[:, 4096:8192], in_v2)
            p.wait("tensor", vsem, vld[kt])
            p.wait("tensor", vsem2, vld2)
            for qb in range(GQ):
                for kc in range(4):
                    col = qb * 16384 + kt * 512 + kc * 128
                    for dvt in range(4):
                        last = (kt == 31 and kc == 3)
                        p.mm(cm.pbk[qb * 4 + dvt][:, :],
                             cm.pa[:, col:col + 128],
                             vbuf[:, kc * 2048 + dvt * 512: kc * 2048 + (dvt + 1) * 512],
                             (kt == 0 and kc == 0), last,
                             sem=(pe if (qb == GQ - 1 and kc == 3 and dvt == 3) else None))
            pv_done[kt] = p.ct[pe]
        p.wait("vector", pe, pv_done[31])
        for qb in range(GQ):
            for dvt in range(4):
                p.chain("vector", dv, lambda e, qb=qb, dvt=dvt: e.tensor_scalar_mul(
                    cm.pb[:, 16384 + qb * 2048 + dvt * 512: 16384 + qb * 2048 + (dvt + 1) * 512],
                    cm.pbk[qb * 4 + dvt][:, :], cm.rden[qb][:, :]))
        bevac_last[g] = p.ct[dv]
        p.barrier(cm.bar)
        for qb in range(GQ):
            p.wait("tensor", dv, p.ct[dv])
            def ev_c(gi, bank_ap, qb=qb):
                out_ap = bass.AP(cm.pb, 20480 + gi * 4 * GQ * 128 + qb * 128,
                                 [[24576, 128], [GQ * 128, 4], [1, 128]])
                p.on("vector", lambda e, o=out_ap, b=bank_ap:
                     e.tensor_copy(o, b).then_inc(dv, 1))
                return p.inc(dv)
            tr_batch(cm, [cm.pb[:, 16384 + qb * 2048 + dvc * 128:
                                 16384 + qb * 2048 + (dvc + 1) * 128]
                          for dvc in range(16)], ev_c)
        fm_done = p.ct[dv]
        trc_pe[g] = p.ct[pe]
        p.wait("tensor", dv, fm_done)
        QW = GQ * 128
        evac2 = {}
        st_done = {}
        for c4 in range(4):
            in_ap = bass.AP(wo_dram.tensor, c4 * 512, [[2048, 128], [128 * 2048, 16], [1, 512]])
            wcol = cm.pb[:, (c4 % 2) * 8192:(c4 % 2) * 8192 + 8192]
            wsem = cm.sl[2 + c4 % 2]
            guard = womm.get((g, c4 - 2), pv_done[31])
            p.wait("sync", pe, guard)
            wt = p.dma("sync", wsem, wcol, in_ap)
            p.wait("tensor", wsem, wt)
            for c in range(4):
                cc = c4 * 4 + c
                bank = 6 + cc % 2
                if cc >= 2:
                    p.wait("tensor", ac, evac2[cc - 2])
                for ci in range(16):
                    p.mm(cm.pbk[bank][:, :QW],
                         wcol[:, ci * 512 + c * 128: ci * 512 + (c + 1) * 128],
                         cm.pb[:, 20480 + ci * QW: 20480 + (ci + 1) * QW],
                         ci == 0, False)
                pet = p.mm(cm.pbk[bank][:, :QW], cm.bias[bo_name][:1, cc * 128:(cc + 1) * 128],
                           cm.ones[:1, :QW], False, True, sem=pe)
                p.wait("scalar", pe, pet)
                p.wait("scalar", cm.ssm[cc % 2], p.ct[cm.ssm[cc % 2]])
                p.on("scalar", lambda e, cc=cc, bank=bank: e.copy(
                    cm.bst[cc % 2][:, :QW], cm.pbk[bank][:, :QW]).then_inc(ac, 1))
                evac2[cc] = p.inc(ac)
                p.wait("gpsimd", ac, evac2[cc])
                st_done[cc] = p.dma("gpsimd", cm.ssm[cc % 2],
                                    z_out_dram[cc * 128:(cc + 1) * 128, g * QW:(g + 1) * QW],
                                    cm.bst[cc % 2][:, :QW])
            womm[(g, c4)] = p.ct[pe]
        evac2_last[g] = evac2[15]
    cm.barrier()


def emit_z4_residual(cm, z4_dram, x1_src, x2_aps):
    p = cm.p
    ld, gq, pe, dv, ac = cm.ld, cm.gq, cm.pe, cm.dv, cm.ac
    for ci in range(16):
        p.dma("sync", cm.zl, cm.pa[:, ci * 2048:(ci + 1) * 2048],
              z4_dram[ci * 128:(ci + 1) * 128, :])
    p.wait("tensor", cm.zl, p.ct[cm.zl])
    st_done = [0, 0]
    xt_free = [0, 0]
    prev_tr = 0
    for t in range(16):
        fst = cm.fstage[t % 2]
        xsem = cm.sl[6 + t % 2]
        p.wait("sync", dv, xt_free[t % 2])
        x1t = p.dma("sync", xsem, cm.xt[t % 2][:, :], x1_src[t * 128:(t + 1) * 128, :])
        p.wait("vector", xsem, x1t)
        if t >= 2:
            p.wait("vector", cm.ssm[4 + t % 2], p.ct[cm.ssm[4 + t % 2]])
        p.wait("tensor", dv, prev_tr)

        def ev_z4(gi, bank_ap, fst=fst, t=t):
            p.on("vector", lambda e, gi=gi, b=bank_ap, fst=fst, t=t:
                 e.scalar_tensor_tensor(fst[:, gi * 512:(gi + 1) * 512], b, 1.0,
                                        cm.xt[t % 2][:, gi * 512:(gi + 1) * 512],
                                        op0=OP.mult, op1=OP.add).then_inc(dv, 1))
            return p.inc(dv)
        prev_tr = tr_batch(cm, [cm.pa[:, dc * 2048 + t * 128: dc * 2048 + (t + 1) * 128]
                                for dc in range(16)], ev_z4)
        xt_free[t % 2] = p.ct[dv]
        p.wait("gpsimd", dv, p.ct[dv])
        for ap_ in x2_aps:
            p.dma("gpsimd", cm.ssm[4 + t % 2], ap_[t * 128:(t + 1) * 128, :], fst[:, :])
    cm.barrier()


def emit_ffn(cm, z_dram, w1b_dram, w3b_dram, w2t_dram, xsrc, xsrc_is_f32, out_aps, dbg_hidden=None):
    p = cm.p
    ld, gq, pe, dv, ac = cm.ld, cm.gq, cm.pe, cm.dv, cm.ac
    st_done = {}
    for tg in range(4):
        in_z = bass.AP(z_dram.tensor, tg * 512, [[2048, 128], [128 * 2048, 16], [1, 512]])
        zg = cm.pb[:, 0:8192]
        zt = p.dma("sync", cm.zl, zg, in_z)
        p.wait("tensor", cm.zl, zt)
        mm_done = {}
        sil_done = {}
        for hc2 in range(32):
            wb1 = cm.pb[:, 8192 + (hc2 % 2) * 4096: 8192 + (hc2 % 2) * 4096 + 4096]
            wb3 = cm.pb[:, 16384 + (hc2 % 2) * 4096: 16384 + (hc2 % 2) * 4096 + 4096]
            blk, off = (hc2 * 256) // 512, (hc2 * 256) % 512
            if hc2 >= 2:
                p.wait("sync", pe, mm_done[2 * (hc2 - 2) + 1])
            in1 = bass.AP(w1b_dram.tensor, blk * 2048 * 512 + off, [[512, 128], [128 * 512, 16], [1, 256]])
            in3 = bass.AP(w3b_dram.tensor, blk * 2048 * 512 + off, [[512, 128], [128 * 512, 16], [1, 256]])
            s1m = cm.sl[0 + hc2 % 2]
            s3m = cm.sl[2 + hc2 % 2]
            t1 = p.dma("sync", s1m, wb1, in1)
            t3 = p.dma("sync", s3m, wb3, in3)
            p.wait("tensor", s1m, t1)
            p.wait("tensor", s3m, t3)
            for h in range(2):
                hc = hc2 * 2 + h
                bg, bl = (hc % 2) * 2, (hc % 2) * 2 + 1
                if hc >= 2:
                    p.wait("tensor", dv, sil_done[hc - 2])
                for ci in range(16):
                    p.mm(cm.pbk[bg][:, :], wb1[:, ci * 256 + h * 128: ci * 256 + (h + 1) * 128],
                         zg[:, ci * 512:(ci + 1) * 512], ci == 0, ci == 15,
                         sem=(pe if ci == 15 else None))
                for ci in range(16):
                    p.mm(cm.pbk[bl][:, :], wb3[:, ci * 256 + h * 128: ci * 256 + (h + 1) * 128],
                         zg[:, ci * 512:(ci + 1) * 512], ci == 0, ci == 15,
                         sem=(pe if ci == 15 else None))
                mm_done[hc] = p.ct[pe]
                p.wait("scalar", pe, mm_done[hc])
                if hc >= 2:
                    p.wait("scalar", dv, sil_done[hc - 2])
                p.on("scalar", lambda e, hc=hc, bg=bg: e.activation(
                    cm.bst[hc % 2][:, :512], cm.pbk[bg][:, :], AF.Silu).then_inc(ac, 1))
                sat = p.inc(ac)
                p.wait("vector", ac, sat)
                p.on("vector", lambda e, hc=hc, bl=bl: e.tensor_mul(
                    cm.pa[:, hc * 512:(hc + 1) * 512], cm.bst[hc % 2][:, :512],
                    cm.pbk[bl][:, :]).then_inc(dv, 1))
                sil_done[hc] = p.inc(dv)
        p.wait("tensor", dv, sil_done[63])
        p.wait("sync", dv, sil_done[63])
        if dbg_hidden is not None and tg == 0:
            p.wait("gpsimd", dv, sil_done[63])
            p.dma("gpsimd", cm.gq, dbg_hidden[:, :], cm.pa[:, :])
        ev_done = 0
        w2mm = {}
        for dt in range(4):
            if dt > 0:
                p.wait("tensor", dv, ev_done)
            for hc in range(64):
                w2buf = cm.bst[1][:, (hc % 2) * 512:(hc % 2) * 512 + 512]
                w2s = cm.sl[4 + hc % 2]
                guard = w2mm.get((dt, hc - 2), w2mm.get((dt - 1, 62 + hc), 0))
                if guard:
                    p.wait("sync", pe, guard)
                in2 = bass.AP(w2t_dram.tensor, (hc * 128) * 2048 + dt * 512,
                              [[2048, 128], [1, 512]])
                wt = p.dma("sync", w2s, w2buf, in2)
                p.wait("tensor", w2s, wt)
                for sub in range(4):
                    p.mm(cm.pbk[4 + sub][:, :],
                         cm.pa[:, hc * 512 + sub * 128: hc * 512 + (sub + 1) * 128],
                         w2buf, hc == 0, hc == 63,
                         sem=(pe if sub == 3 else None))
                w2mm[(dt, hc)] = p.ct[pe]
            p.wait("vector", pe, w2mm[(dt, 63)])
            for sub in range(4):
                row0 = tg * 512 + sub * 128
                xr = cm.xres[sub % 2]
                xq = "sync" if xsrc_is_f32 else "gpsimd"
                xqs = cm.sl[6 + sub % 2]
                p.wait(xq, dv, st_done.get(("xr", sub % 2), 0))
                xt = p.dma(xq, xqs, xr[:, :], xsrc[row0:row0 + 128, dt * 512:(dt + 1) * 512])
                p.wait("vector", xqs, xt)
                fsem = cm.ssm[4 + sub % 2]
                if p.ct[fsem]:
                    p.wait("vector", fsem, p.ct[fsem])
                p.on("vector", lambda e, sub=sub, xr=xr: e.scalar_tensor_tensor(
                    cm.fstage[sub % 2][:, :512], cm.pbk[4 + sub][:, :], 1.0, xr[:, :],
                    op0=OP.mult, op1=OP.add).then_inc(dv, 1))
                dvt = p.inc(dv)
                st_done[("xr", sub % 2)] = dvt
                p.wait("gpsimd", dv, dvt)
                for ap_ in out_aps:
                    p.dma("gpsimd", fsem, ap_[row0:row0 + 128, dt * 512:(dt + 1) * 512],
                          cm.fstage[sub % 2][:, :512])
            ev_done = p.ct[dv]
        cm.barrier()



def build_l1():
    nc = bass.Bass("TRN2", target_bir_lowering=False, detect_race_conditions=False)
    st = contextlib.ExitStack()
    x0 = nc.dram_tensor("x0", [T, D], BF16, kind="ExternalInput").ap()
    w1b = nc.dram_tensor("w1b", [16 * D, 512], BF16, kind="ExternalInput").ap()
    w3b = nc.dram_tensor("w3b", [16 * D, 512], BF16, kind="ExternalInput").ap()
    w2t = nc.dram_tensor("w2t", [HD, D], BF16, kind="ExternalInput").ap()
    wq = nc.dram_tensor("wq", [D, D], BF16, kind="ExternalInput").ap()
    wk = nc.dram_tensor("wk", [D, D], BF16, kind="ExternalInput").ap()
    wv = nc.dram_tensor("wv", [D, D], BF16, kind="ExternalInput").ap()
    bq = nc.dram_tensor("bq", [1, D], BF16, kind="ExternalInput").ap()
    bk = nc.dram_tensor("bk", [1, D], BF16, kind="ExternalInput").ap()
    bv = nc.dram_tensor("bv", [1, D], BF16, kind="ExternalInput").ap()
    idin = nc.dram_tensor("idin", [128, 128], BF16, kind="ExternalInput").ap()
    onein = nc.dram_tensor("onein", [1, 512], BF16, kind="ExternalInput").ap()
    x1 = nc.dram_tensor("x1", [T, D], F32, kind="ExternalOutput").ap()
    q = nc.dram_tensor("q", [D, T], BF16, kind="ExternalOutput").ap()
    k = nc.dram_tensor("k", [D, T], BF16, kind="ExternalOutput").ap()
    v = nc.dram_tensor("v", [T, D], BF16, kind="ExternalOutput").ap()
    zd = nc.dram_tensor("zd", [D, T], BF16, kind="Internal").ap()
    x1i = nc.dram_tensor("x1i", [T, D], F32, kind="Internal").ap()

    p = P(nc)
    cm = CM(nc, st, p)
    cm.load_consts(idin, onein, {"bq": bq, "bk": bk, "bv": bv})
    emit_ln(cm, x0, zd, src_is_f32=False)
    emit_ffn(cm, zd, w1b, w3b, w2t, x0, False, [x1, x1i])
    emit_ln(cm, x1i, zd, src_is_f32=True)
    emit_projs(cm, zd, [(wq, "bq", q, QSCALE, False), (wk, "bk", k, 1.0, False),
                        (wv, "bv", v, 1.0, True)])
    cm.final_waits()
    p.emit()
    st.close()
    return nc


def build_attn(last):
    nc = bass.Bass("TRN2", target_bir_lowering=False, detect_race_conditions=False)
    st = contextlib.ExitStack()
    qi = nc.dram_tensor("qi", [D, T], BF16, kind="ExternalInput").ap()
    kg = nc.dram_tensor("kg", [NTOK, D], BF16, kind="ExternalInput").ap()
    vg = nc.dram_tensor("vg", [NTOK, D], BF16, kind="ExternalInput").ap()
    wo = nc.dram_tensor("wo", [D, D], BF16, kind="ExternalInput").ap()
    bo = nc.dram_tensor("bo", [1, D], BF16, kind="ExternalInput").ap()
    idin = nc.dram_tensor("idin", [128, 128], BF16, kind="ExternalInput").ap()
    onein = nc.dram_tensor("onein", [1, 512], BF16, kind="ExternalInput").ap()
    zd = nc.dram_tensor("zd", [D, T], BF16, kind="Internal").ap()

    p = P(nc)
    cm = CM(nc, st, p)
    if not last:
        wq = nc.dram_tensor("wq", [D, D], BF16, kind="ExternalInput").ap()
        wk = nc.dram_tensor("wk", [D, D], BF16, kind="ExternalInput").ap()
        wv = nc.dram_tensor("wv", [D, D], BF16, kind="ExternalInput").ap()
        bq = nc.dram_tensor("bq", [1, D], BF16, kind="ExternalInput").ap()
        bk = nc.dram_tensor("bk", [1, D], BF16, kind="ExternalInput").ap()
        bv = nc.dram_tensor("bv", [1, D], BF16, kind="ExternalInput").ap()
        q = nc.dram_tensor("q", [D, T], BF16, kind="ExternalOutput").ap()
        k = nc.dram_tensor("k", [D, T], BF16, kind="ExternalOutput").ap()
        v = nc.dram_tensor("v", [T, D], BF16, kind="ExternalOutput").ap()
        cm.load_consts(idin, onein, {"bo": bo, "bq": bq, "bk": bk, "bv": bv})
        emit_attn_layer(cm, qi, kg, vg, wo, "bo", zd)
        emit_projs(cm, zd, [(wq, "bq", q, QSCALE, False), (wk, "bk", k, 1.0, False),
                            (wv, "bv", v, 1.0, True)])
    else:
        x1 = nc.dram_tensor("x1", [T, D], F32, kind="ExternalInput").ap()
        w1b = nc.dram_tensor("w1b", [16 * D, 512], BF16, kind="ExternalInput").ap()
        w3b = nc.dram_tensor("w3b", [16 * D, 512], BF16, kind="ExternalInput").ap()
        w2t = nc.dram_tensor("w2t", [HD, D], BF16, kind="ExternalInput").ap()
        y = nc.dram_tensor("y", [T, D], F32, kind="ExternalOutput").ap()
        x2i = nc.dram_tensor("x2i", [T, D], F32, kind="Internal").ap()
        cm.load_consts(idin, onein, {"bo": bo})
        emit_attn_layer(cm, qi, kg, vg, wo, "bo", zd)
        emit_z4_residual(cm, zd, x1, [x2i])
        emit_ln(cm, x2i, zd, src_is_f32=True)
        emit_ffn(cm, zd, w1b, w3b, w2t, x2i, True, [y])
    cm.final_waits()
    p.emit()
    st.close()
    return nc



_CACHE = {}


def _get(name, fn):
    if name not in _CACHE:
        _CACHE[name] = fn()
    return _CACHE[name]


def _run(nc, maps):
    return run_bass_kernel_spmd(nc, maps, core_ids=list(range(NC)))


def kernel(L0_attn_x, L0_attn_Wq, L0_attn_bq, L0_attn_Wk, L0_attn_bk,
           L0_attn_Wv, L0_attn_bv, L0_attn_Wo, L0_attn_bo,
           L0_ffn_W1, L0_ffn_W2, L0_ffn_W3):
    x = np.ascontiguousarray(np.asarray(L0_attn_x).reshape(NTOK, D))
    Wq = np.asarray(L0_attn_Wq)
    Wk = np.asarray(L0_attn_Wk)
    Wv = np.asarray(L0_attn_Wv)
    Wo = np.asarray(L0_attn_Wo)
    bq = np.asarray(L0_attn_bq)
    bk = np.asarray(L0_attn_bk)
    bv = np.asarray(L0_attn_bv)
    bo = np.asarray(L0_attn_bo)
    W1 = np.asarray(L0_ffn_W1)
    W2 = np.asarray(L0_ffn_W2)
    W3 = np.asarray(L0_ffn_W3)

    ident = np.ascontiguousarray(np.eye(128, dtype=np.float32).astype(bfdt))
    ones = np.ones((1, 512), dtype=bfdt)

    def blkT(W):
        Wt = np.ascontiguousarray(W.T)
        return np.ascontiguousarray(
            Wt.reshape(D, 16, 512).transpose(1, 0, 2).reshape(16 * D, 512))

    W1Tb = blkT(W1)
    W3Tb = blkT(W3)

    nc0 = _get("l0", build_l0)
    maps0 = []
    for c in range(NC):
        maps0.append({
            "w1s": np.ascontiguousarray(W1Tb[c * 2 * D:(c + 1) * 2 * D]),
            "w3s": np.ascontiguousarray(W3Tb[c * 2 * D:(c + 1) * 2 * D]),
            "w2s": np.ascontiguousarray(W2[c * 256:(c + 1) * 256]),
            "idin": ident,
        })
    r0 = _run(nc0, maps0)
    w1b = np.concatenate([r0.results[c]["w1b"] for c in range(NC)], axis=0)
    w3b = np.concatenate([r0.results[c]["w3b"] for c in range(NC)], axis=0)
    w2t = np.concatenate([r0.results[c]["w2t"] for c in range(NC)], axis=1)
    w1b = np.ascontiguousarray(w1b)
    w3b = np.ascontiguousarray(w3b)
    w2t = np.ascontiguousarray(w2t)

    nc1 = _get("l1", build_l1)
    maps1 = []
    for c in range(NC):
        maps1.append({
            "x0": np.ascontiguousarray(x[c * T:(c + 1) * T]),
            "w1b": w1b, "w3b": w3b, "w2t": w2t,
            "wq": np.ascontiguousarray(Wq[0]), "wk": np.ascontiguousarray(Wk[0]),
            "wv": np.ascontiguousarray(Wv[0]),
            "bq": np.ascontiguousarray(bq[0:1]), "bk": np.ascontiguousarray(bk[0:1]),
            "bv": np.ascontiguousarray(bv[0:1]),
            "idin": ident, "onein": ones,
        })
    r1 = _run(nc1, maps1)
    x1 = [r1.results[c]["x1"] for c in range(NC)]
    qs = [r1.results[c]["q"] for c in range(NC)]
    kgl = np.ascontiguousarray(np.concatenate([r1.results[c]["k"] for c in range(NC)], axis=0))
    vgl = np.ascontiguousarray(np.concatenate([r1.results[c]["v"] for c in range(NC)], axis=0))

    nca = _get("lattn", lambda: build_attn(False))
    for i in range(3):
        maps = []
        for c in range(NC):
            maps.append({
                "qi": np.ascontiguousarray(qs[c]), "kg": kgl, "vg": vgl,
                "wo": np.ascontiguousarray(Wo[i]), "bo": np.ascontiguousarray(bo[i:i + 1]),
                "wq": np.ascontiguousarray(Wq[i + 1]), "wk": np.ascontiguousarray(Wk[i + 1]),
                "wv": np.ascontiguousarray(Wv[i + 1]),
                "bq": np.ascontiguousarray(bq[i + 1:i + 2]),
                "bk": np.ascontiguousarray(bk[i + 1:i + 2]),
                "bv": np.ascontiguousarray(bv[i + 1:i + 2]),
                "idin": ident, "onein": ones,
            })
        r = _run(nca, maps)
        qs = [r.results[c]["q"] for c in range(NC)]
        kgl = np.ascontiguousarray(np.concatenate([r.results[c]["k"] for c in range(NC)], axis=0))
        vgl = np.ascontiguousarray(np.concatenate([r.results[c]["v"] for c in range(NC)], axis=0))

    ncl = _get("llast", lambda: build_attn(True))
    maps = []
    for c in range(NC):
        maps.append({
            "qi": np.ascontiguousarray(qs[c]), "kg": kgl, "vg": vgl,
            "wo": np.ascontiguousarray(Wo[3]), "bo": np.ascontiguousarray(bo[3:4]),
            "x1": np.ascontiguousarray(x1[c]),
            "w1b": w1b, "w3b": w3b, "w2t": w2t,
            "idin": ident, "onein": ones,
        })
    r = _run(ncl, maps)
    y = np.concatenate([r.results[c]["y"] for c in range(NC)], axis=0)
    return y.reshape(1, NTOK, D).astype(np.float32)


# version 05 (final)
# speedup vs baseline: 1.0998x; 1.0998x over previous
import contextlib
import numpy as np
import ml_dtypes
import concourse.bass as bass
import concourse.mybir as mybir
from concourse.bass_utils import run_bass_kernel_spmd

BF16 = mybir.dt.bfloat16
F32 = mybir.dt.float32
AF = mybir.ActivationFunctionType
OP = mybir.AluOpType
AX = mybir.AxisListType
bfdt = ml_dtypes.bfloat16

NC = 8
T = 2048
D = 2048
HD = 8192
NTOK = 16384
QSCALE = 1.0 / float(np.sqrt(2048.0))
FWHT_SCALE = 0.5 / float(np.sqrt(8192.0))
EPS = 1e-5
GQ = 2

ENGINES = ("sync", "gpsimd", "vector", "scalar", "tensor")


class P:

    def __init__(self, nc):
        self.nc = nc
        self.ops = []
        self.ct = {}
        self.last_wait = {}

    def sem(self, name):
        h = self.nc.alloc_semaphore(name=name)
        self.ct[h] = 0
        return h

    def inc(self, h, n=1):
        self.ct[h] += n
        return self.ct[h]

    def on(self, eng, fn):
        self.ops.append((eng, fn))

    def dma(self, qeng, sem, out, in_):
        tgt = self.inc(sem, 16)
        self.on(qeng, lambda e, o=out, i=in_, s=sem: e.dma_start(out=o, in_=i).then_inc(s, 16))
        return tgt

    def wait(self, eng, sem, tgt):
        if tgt <= 0:
            return
        key = (eng, sem)
        if self.last_wait.get(key, -1) >= tgt:
            return
        self.last_wait[key] = tgt
        self.on(eng, lambda e, s=sem, t=tgt: e.wait_ge(s, t))

    def barrier(self, bar):
        tgt = self.ct[bar] + 5
        for en in ENGINES:
            self.on(en, lambda e, b=bar: e.sem_inc(b, 1))
        self.ct[bar] = tgt
        for en in ENGINES:
            self.wait(en, bar, tgt)

    def chain(self, eng, sem, fn):
        t = self.ct[sem]
        def g(e, fn=fn, s=sem, t=t):
            if t:
                e.wait_ge(s, t)
            fn(e).then_inc(s, 1)
        self.on(eng, g)
        return self.inc(sem)

    def mm(self, out, lhsT, rhs, start, stop, sem=None):
        tgt = self.inc(sem) if sem is not None else None
        def f(e, o=out, l=lhsT, r=rhs, st=start, sp=stop, s=sem):
            ins = e.matmul(o, l, r, start=st, stop=sp)
            if s is not None:
                ins.then_inc(s, 1)
        self.on("tensor", f)
        return tgt

    def emit(self):
        with self.nc.Block() as block:
            for name in ENGINES:
                def run(e, name=name):
                    for n, fn in self.ops:
                        if n == name:
                            fn(e)
                getattr(block, name)(run)


def tr_batch(cm, srcs, evac_fn, banks=(0, 1), pre_wait=None):
    p = cm.p
    ev = []
    ngroups = (len(srcs) + 3) // 4
    for gi in range(ngroups):
        bank = banks[gi % 2]
        if pre_wait is not None:
            p.wait("tensor", cm.dv, pre_wait(gi))
        if gi >= 2:
            p.wait("tensor", cm.dv, ev[gi - 2])
        chunk = srcs[gi * 4:(gi + 1) * 4]
        for j, s in enumerate(chunk):
            p.mm(cm.pbk[bank][:, j * 128:(j + 1) * 128], s, cm.ident[:, :],
                 True, True, sem=(cm.pe if j == len(chunk) - 1 else None))
        p.wait("vector", cm.pe, p.ct[cm.pe])
        ev.append(evac_fn(gi, cm.pbk[bank][:, 0:len(chunk) * 128]))
    return ev[-1]


def sb(nc, st, name, shape, dtype):
    return st.enter_context(nc.sbuf_tensor(name, shape, dtype))


def ps(nc, st, name, shape, dtype=F32):
    return st.enter_context(nc.psum_tensor(name, shape, dtype))



def build_l0():
    nc = bass.Bass("TRN2", target_bir_lowering=False, detect_race_conditions=False)
    st = contextlib.ExitStack()
    w1s = nc.dram_tensor("w1s", [2 * D, 512], F32, kind="ExternalInput").ap()
    w3s = nc.dram_tensor("w3s", [2 * D, 512], F32, kind="ExternalInput").ap()
    w2s = nc.dram_tensor("w2s", [256, HD], F32, kind="ExternalInput").ap()
    idin = nc.dram_tensor("idin", [128, 128], BF16, kind="ExternalInput").ap()
    w1b = nc.dram_tensor("w1b", [2 * D, 512], BF16, kind="ExternalOutput").ap()
    w3b = nc.dram_tensor("w3b", [2 * D, 512], BF16, kind="ExternalOutput").ap()
    w2t = nc.dram_tensor("w2t", [HD, 256], BF16, kind="ExternalOutput").ap()

    p = P(nc)
    ld, gs, pe, dv, bar = (p.sem(n) for n in ("ld", "gs", "pe", "dv", "bar"))

    cast = sb(nc, st, "cast", [128, 4096], BF16)
    fa = sb(nc, st, "fa", [128, HD], F32)
    fb = sb(nc, st, "fb", [128, HD], F32)
    fbf = sb(nc, st, "fbf", [128, HD], BF16)
    ident = sb(nc, st, "ident", [128, 128], BF16)
    tst = sb(nc, st, "tst", [128, 64 * 128], BF16)
    pt = [ps(nc, st, f"pt{i}", [128, 512], F32) for i in range(2)]

    idt = p.dma("sync", ld, ident[:, :], idin)

    for src, dst in ((w1s, w1b), (w3s, w3b)):
        for blk in range(4):
            in_ap = bass.AP(src.tensor, blk * 1024 * 512, [[512, 128], [128 * 512, 8], [1, 512]])
            p.wait("gpsimd", gs, p.ct[gs])
            t = p.dma("gpsimd", gs, cast[:, :], in_ap)
            p.wait("gpsimd", gs, t)
            out_ap = bass.AP(dst.tensor, blk * 1024 * 512, [[512, 128], [128 * 512, 8], [1, 512]])
            p.dma("gpsimd", gs, out_ap, cast[:, :])

    for blk in range(2):
        if blk == 1:
            p.wait("sync", dv, p.ct[dv])
            p.wait("vector", gs, p.ct[gs])
        t = p.dma("sync", ld, fa[:, :], w2s[blk * 128:(blk + 1) * 128, :])
        p.wait("vector", ld, t)
        src, dst = fa, fb
        h = 1
        while h < HD:
            nblk = HD // (2 * h)
            a_in = bass.AP(src, 0, [[HD, 128], [2 * h, nblk], [1, h]])
            b_in = bass.AP(src, h, [[HD, 128], [2 * h, nblk], [1, h]])
            a_out = bass.AP(dst, 0, [[HD, 128], [2 * h, nblk], [1, h]])
            b_out = bass.AP(dst, h, [[HD, 128], [2 * h, nblk], [1, h]])
            p.on("vector", lambda e, o=a_out, x=a_in, y=b_in: e.tensor_add(o, x, y))
            p.on("vector", lambda e, o=b_out, x=a_in, y=b_in:
                 e.tensor_sub(o, x, y).then_inc(dv, 1))
            p.wait("vector", dv, p.inc(dv))
            src, dst = dst, src
            h *= 2
        def scl(e, s=src):
            e.tensor_scalar_mul(fbf[:, :], s[:, :], FWHT_SCALE).then_inc(dv, 1)
        p.on("vector", scl)
        dvt = p.inc(dv)
        p.wait("tensor", dv, dvt)
        p.wait("tensor", ld, p.ct[ld])
        evt = []
        for hc in range(64):
            bank = hc % 2
            if hc >= 2:
                p.wait("tensor", dv, evt[hc - 2])
            p.mm(pt[bank][:, 0:128],
                 fbf[:, hc * 128:(hc + 1) * 128], ident[:, :], True, True, sem=pe)
            p.wait("vector", pe, p.ct[pe])
            p.on("vector", lambda e, hc=hc, bank=bank:
                 e.tensor_copy(tst[:, hc * 128:(hc + 1) * 128],
                               pt[bank][:, 0:128]).then_inc(dv, 1))
            evt.append(p.inc(dv))
        p.wait("gpsimd", dv, p.ct[dv])
        out_ap = bass.AP(w2t.tensor, blk * 128, [[256, 128], [128 * 256, 64], [1, 128]])
        p.dma("gpsimd", gs, out_ap, tst[:, :])
        p.barrier(bar)

    p.wait("gpsimd", gs, p.ct[gs])
    p.wait("sync", ld, p.ct[ld])
    p.emit()
    st.close()
    return nc



class CM:
    def __init__(self, nc, st, p):
        self.nc, self.p, self.st = nc, p, st
        self.pa = sb(nc, st, "pa", [128, 32768], BF16)
        self.pb = sb(nc, st, "pb", [128, 24576], BF16)
        self.xt = [sb(nc, st, f"xt{i}", [128, 2048], F32) for i in range(2)]
        self.sq = sb(nc, st, "sq", [128, 2048], F32)
        self.fstage = [sb(nc, st, f"fstage{i}", [128, 2048], F32) for i in range(2)]
        self.xres = [sb(nc, st, f"xres{i}", [128, 512], F32) for i in range(2)]
        self.xn = sb(nc, st, "xn", [128, 2048], BF16)
        self.zst = [sb(nc, st, f"zst{i}", [128, 2048], BF16) for i in range(2)]
        self.bst = [sb(nc, st, f"bst{i}", [128, 2048], BF16) for i in range(2)]
        self.s1 = sb(nc, st, "s1", [128, 1], F32)
        self.s2 = sb(nc, st, "s2", [128, 1], F32)
        self.s3 = sb(nc, st, "s3", [128, 1], F32)
        self.s4 = sb(nc, st, "s4", [128, 1], F32)
        self.nm = [sb(nc, st, f"nm{i}", [128, 32], F32) for i in range(GQ)]
        self.rs = [sb(nc, st, f"rs{i}", [128, 32], F32) for i in range(GQ)]
        self.fc = [sb(nc, st, f"fc{i}", [128, 32], F32) for i in range(GQ)]
        self.den = [sb(nc, st, f"den{i}", [128, 1], F32) for i in range(GQ)]
        self.rden = [sb(nc, st, f"rden{i}", [128, 1], F32) for i in range(GQ)]
        self.ident = sb(nc, st, "ident", [128, 128], BF16)
        self.ones = sb(nc, st, "ones", [1, 512], BF16)
        self.bias = {}
        self.pbk = [ps(nc, st, f"pbk{i}", [128, 512], F32) for i in range(8)]
        self.ld = p.sem("ld")
        self.gq = p.sem("gq")
        self.cs = p.sem("cs")
        self.zl = p.sem("zl")
        self.sl = [p.sem(f"sl{i}") for i in range(8)]
        self.ssm = [p.sem(f"ss{i}") for i in range(6)]
        self.pe = p.sem("pe")
        self.dv = p.sem("dv")
        self.ac = p.sem("ac")
        self.bar = p.sem("bar")

    def final_waits(self):
        p = self.p
        for s in [self.ld, self.gq, self.cs, self.zl] + self.sl + self.ssm:
            if p.ct[s]:
                p.wait("gpsimd", s, p.ct[s])

    def barrier(self):
        p = self.p
        for s in self.ssm + [self.gq]:
            if p.ct[s]:
                p.wait("gpsimd", s, p.ct[s])
        p.barrier(self.bar)

    def load_consts(self, idin, onein, bias_rows):
        p = self.p
        p.dma("sync", self.cs, self.ident[:, :], idin)
        p.dma("sync", self.cs, self.ones[:1, :], onein)
        for nm, ap in bias_rows.items():
            t = sb(self.nc, self.st, f"bias_{nm}", [1, 2048], BF16)
            self.bias[nm] = t
            p.dma("sync", self.cs, t[:1, :], ap)
        tgt = p.ct[self.cs]
        for en in ("tensor", "vector", "scalar", "sync"):
            p.wait(en, self.cs, tgt)


def emit_ln(cm, x_src, z_dram, src_is_f32):
    p = cm.p
    ld, gq, pe, dv, ac = cm.ld, cm.gq, cm.pe, cm.dv, cm.ac
    store_tgt = [0, 0]
    prev_var = 0
    prev_tp_done = 0
    prev_tr = 0
    xt_guard = [(0, 0), (0, 0)]
    for t in range(16):
        xt = cm.xt[t % 2] if src_is_f32 else cm.bst[t % 2]
        xsem = cm.sl[t % 2]
        gdv, gac = xt_guard[t % 2]
        p.wait("sync", dv, gdv)
        p.wait("sync", ac, gac)
        lt = p.dma("sync", xsem, xt[:, :], x_src[t * 128:(t + 1) * 128, :])
        p.wait("vector", xsem, lt)
        p.wait("scalar", xsem, lt)
        p.chain("vector", dv, lambda e, xt=xt: e.reduce_sum(cm.s1[:, :], xt[:, :],
                                                            axis=AX.X, negate=True))
        p.chain("vector", dv, lambda e: e.tensor_scalar_mul(cm.s1[:, :], cm.s1[:, :], 1.0 / 2048.0))
        p.wait("scalar", dv, prev_var)
        act = p.chain("scalar", ac, lambda e, xt=xt: e.activation(
            cm.sq[:, :], xt[:, :], AF.Square, accum_out=cm.s2[:, :]))
        p.wait("vector", ac, act)
        p.chain("vector", dv, lambda e: e.tensor_mul(cm.s3[:, :], cm.s1[:, :], cm.s1[:, :]))
        p.chain("vector", dv, lambda e: e.tensor_scalar(cm.s2[:, :], cm.s2[:, :], 1.0 / 2048.0,
                                                        EPS, op0=OP.mult, op1=OP.add))
        prev_var = p.chain("vector", dv, lambda e: e.tensor_sub(cm.s2[:, :], cm.s2[:, :],
                                                                cm.s3[:, :]))
        p.wait("scalar", dv, prev_var)
        p.chain("scalar", ac, lambda e: e.activation(cm.s3[:, :], cm.s2[:, :], AF.Ln))
        act2 = p.chain("scalar", ac, lambda e: e.activation(cm.s4[:, :], cm.s3[:, :], AF.Exp,
                                                            scale=-0.5))
        p.wait("vector", ac, act2)
        p.wait("vector", pe, prev_tp_done)
        dvt2 = p.chain("vector", dv, lambda e, xt=xt: e.tensor_scalar(
            cm.xn[:, :], xt[:, :], cm.s1[:, :], cm.s4[:, :], op0=OP.add, op1=OP.mult))
        xt_guard[t % 2] = (dvt2, act)
        zstg = cm.zst[t % 2]
        p.wait("tensor", dv, dvt2)
        p.wait("tensor", dv, prev_tr)
        if t >= 2:
            p.wait("vector", cm.ssm[2 + t % 2], p.ct[cm.ssm[2 + t % 2]])

        def ev_ln(gi, bank_ap, zstg=zstg):
            p.on("vector", lambda e, gi=gi, b=bank_ap, zstg=zstg:
                 e.tensor_copy(zstg[:, gi * 512:(gi + 1) * 512], b).then_inc(dv, 1))
            return p.inc(dv)
        prev_tr = tr_batch(cm, [cm.xn[:, dc * 128:(dc + 1) * 128] for dc in range(16)], ev_ln)
        prev_tp_done = p.ct[pe]
        p.wait("gpsimd", dv, p.ct[dv])
        out_ap = bass.AP(z_dram.tensor, t * 128, [[2048, 128], [128 * 2048, 16], [1, 128]])
        store_tgt[t % 2] = p.dma("gpsimd", cm.ssm[2 + t % 2], out_ap, zstg[:, :])
    cm.barrier()


def emit_projs(cm, z_dram, projs):
    p = cm.p
    ld, gq, pe, dv, ac = cm.ld, cm.gq, cm.pe, cm.dv, cm.ac
    for ci in range(16):
        p.dma("sync", cm.zl, cm.pa[:, ci * 2048:(ci + 1) * 2048],
              z_dram[ci * 128:(ci + 1) * 128, :])
    p.wait("tensor", cm.zl, p.ct[cm.zl])
    evac_done = {}
    store_done = {}
    mm_fin = {}
    prev_tr = 0
    for pi, (w_dram, bias_name, out_dram, scale, v_transpose) in enumerate(projs):
        for c4 in range(4):
            j = pi * 4 + c4
            in_ap = bass.AP(w_dram.tensor, c4 * 512, [[2048, 128], [128 * 2048, 16], [1, 512]])
            wcol = cm.pb[:, (j % 2) * 8192:(j % 2) * 8192 + 8192]
            wsem = cm.sl[2 + j % 2]
            if j >= 2:
                p.wait("sync", pe, mm_fin[j - 2])
            wt = p.dma("sync", wsem, wcol, in_ap)
            p.wait("tensor", wsem, wt)
            for c in range(4):
                cc = c4 * 4 + c
                jj = pi * 16 + cc
                bs = (jj % 2) * 4
                if jj >= 2:
                    for t4 in range(4):
                        p.wait("tensor", ac, evac_done[(jj - 2, t4)])
                if v_transpose:
                    p.wait("tensor", dv, prev_tr)
                for ci in range(16):
                    for t4 in range(4):
                        p.mm(cm.pbk[bs + t4][:, :],
                             wcol[:, ci * 512 + c * 128: ci * 512 + (c + 1) * 128],
                             cm.pa[:, ci * 2048 + t4 * 512: ci * 2048 + (t4 + 1) * 512],
                             ci == 0, False)
                stg = cm.bst[jj % 2]
                for t4 in range(4):
                    pet = p.mm(cm.pbk[bs + t4][:, :],
                               cm.bias[bias_name][:1, cc * 128:(cc + 1) * 128],
                               cm.ones[:1, :512], False, True, sem=pe)
                    p.wait("scalar", pe, pet)
                    if t4 == 0 and jj >= 2:
                        p.wait("scalar", cm.ssm[jj % 2], p.ct[cm.ssm[jj % 2]])
                    p.on("scalar", lambda e, t4=t4, stg=stg, s=scale, bs=bs:
                         e.mul(stg[:, t4 * 512:(t4 + 1) * 512],
                               cm.pbk[bs + t4][:, :], s).then_inc(ac, 1))
                    evac_done[(jj, t4)] = p.inc(ac)
                if not v_transpose:
                    p.wait("gpsimd", ac, evac_done[(jj, 3)])
                    store_done[jj] = p.dma("gpsimd", cm.ssm[jj % 2],
                                           out_dram[cc * 128:(cc + 1) * 128, :], stg[:, :])
                else:
                    p.wait("tensor", ac, evac_done[(jj, 3)])
                    tstg = cm.zst[jj % 2]
                    p.wait("tensor", dv, prev_tr)
                    if p.ct[cm.ssm[2 + jj % 2]]:
                        p.wait("vector", cm.ssm[2 + jj % 2], p.ct[cm.ssm[2 + jj % 2]])

                    def ev_v(gi, bank_ap, tstg=tstg):
                        p.on("vector", lambda e, gi=gi, b=bank_ap, tstg=tstg:
                             e.tensor_copy(tstg[:, gi * 512:(gi + 1) * 512], b).then_inc(dv, 1))
                        return p.inc(dv)
                    prev_tr = tr_batch(cm, [stg[:, tt * 128:(tt + 1) * 128] for tt in range(16)],
                                       ev_v, banks=(bs + 2, bs + 3))
                    p.wait("gpsimd", dv, p.ct[dv])
                    out_ap = bass.AP(out_dram.tensor, cc * 128,
                                     [[2048, 128], [128 * 2048, 16], [1, 128]])
                    store_done[jj] = p.dma("gpsimd", cm.ssm[2 + jj % 2], out_ap, tstg[:, :])
            mm_fin[j] = p.ct[pe]
    cm.barrier()


def emit_attn_layer(cm, q_dram, kg_dram, vg_dram, wo_dram, bo_name, z_out_dram, dbg=None):
    p = cm.p
    ld, gq, pe, dv, ac = cm.ld, cm.gq, cm.pe, cm.dv, cm.ac
    NG = 16 // GQ
    womm = {}
    trc_pe = {}
    evac2_last = {}
    a5_last = {}
    bevac_last = {}
    for g in range(NG):
        q0 = g * GQ * 128
        qsb = cm.pb[:, 16384:16384 + 16 * GQ * 128]
        in_q = bass.AP(q_dram.tensor, q0, [[2048, 128], [128 * 2048, 16], [1, GQ * 128]])
        p.wait("sync", pe, trc_pe.get(g - 1, 0))
        qt = p.dma("sync", cm.zl, qsb, in_q)
        p.wait("tensor", cm.zl, qt)
        p.wait("tensor", dv, bevac_last.get(g - 1, 0))
        smm_done = {}
        exp_done = {}
        for kt in range(32):
            r, ltile = kt // 4, kt % 4
            kbuf = cm.pb[:, (kt % 2) * 8192:(kt % 2) * 8192 + 8192]
            if kt >= 2:
                p.wait("sync", pe, smm_done[(kt - 2, GQ - 1)])
                p.wait("gpsimd", pe, smm_done[(kt - 2, GQ - 1)])
            else:
                p.wait("sync", pe, womm.get((g - 1, 3), 0))
                p.wait("gpsimd", pe, womm.get((g - 1, 3), 0))
            in_k1 = bass.AP(kg_dram.tensor, (r * 2048) * 2048 + ltile * 512,
                            [[2048, 128], [128 * 2048, 8], [1, 512]])
            in_k2 = bass.AP(kg_dram.tensor, (r * 2048 + 8 * 128) * 2048 + ltile * 512,
                            [[2048, 128], [128 * 2048, 8], [1, 512]])
            ksem = cm.sl[4 + kt % 2]
            ksem2 = cm.sl[6 + kt % 2]
            ktg = p.dma("sync", ksem, kbuf[:, 0:4096], in_k1)
            ktg2 = p.dma("gpsimd", ksem2, kbuf[:, 4096:8192], in_k2)
            p.wait("tensor", ksem, ktg)
            p.wait("tensor", ksem2, ktg2)
            for qb in range(GQ):
                sbank = cm.pbk[qb]
                if kt >= 1:
                    p.wait("tensor", ac, exp_done[(kt - 1, qb)])
                for ci in range(16):
                    p.mm(sbank[:, :],
                         cm.pb[:, 16384 + ci * GQ * 128 + qb * 128:
                               16384 + ci * GQ * 128 + (qb + 1) * 128],
                         kbuf[:, ci * 512:(ci + 1) * 512],
                         ci == 0, ci == 15, sem=(pe if ci == 15 else None))
                smm_done[(kt, qb)] = p.ct[pe]
                p.wait("vector", pe, smm_done[(kt, qb)])
                p.chain("vector", dv, lambda e, sbank=sbank:
                        e.reduce_max(cm.s1[:, :], sbank[:, :], axis=AX.X, negate=True))
                if kt == 0:
                    dvm = p.chain("vector", dv, lambda e, qb=qb:
                                  e.tensor_copy(cm.nm[qb][:, 0:1], cm.s1[:, :]))
                else:
                    dvm = p.chain("vector", dv, lambda e, qb=qb, kt=kt:
                                  e.tensor_tensor(cm.nm[qb][:, kt:kt + 1],
                                                  cm.nm[qb][:, kt - 1:kt],
                                                  cm.s1[:, :], op=OP.min))
                p.wait("scalar", dv, dvm)
                p.on("scalar", lambda e, qb=qb, kt=kt, sbank=sbank:
                     e.activation(cm.pa[:, qb * 16384 + kt * 512: qb * 16384 + (kt + 1) * 512],
                                  sbank[:, :], AF.Exp, bias=cm.nm[qb][:, kt:kt + 1],
                                  accum_out=cm.rs[qb][:, kt:kt + 1]).then_inc(ac, 1))
                exp_done[(kt, qb)] = p.inc(ac)
        for qb in range(GQ):
            fct = p.chain("scalar", ac, lambda e, qb=qb: e.activation(
                cm.fc[qb][:, :], cm.nm[qb][:, :], AF.Exp, scale=-1.0,
                bias=cm.nm[qb][:, 31:32]))
            p.wait("vector", ac, fct)
            p.chain("vector", dv, lambda e, qb=qb: e.tensor_mul(cm.nm[qb][:, :],
                                                                cm.rs[qb][:, :], cm.fc[qb][:, :]))
            p.chain("vector", dv, lambda e, qb=qb: e.reduce_sum(cm.den[qb][:, :],
                                                                cm.nm[qb][:, :], axis=AX.X))
            rd_done = p.chain("vector", dv, lambda e, qb=qb:
                              e.reciprocal(cm.rden[qb][:, :], cm.den[qb][:, :]))
            corr_tgts = []
            for kt in range(32):
                def corr(e, qb=qb, kt=kt):
                    e.tensor_scalar_mul(
                        cm.pa[:, qb * 16384 + kt * 512: qb * 16384 + (kt + 1) * 512],
                        cm.pa[:, qb * 16384 + kt * 512: qb * 16384 + (kt + 1) * 512],
                        cm.fc[qb][:, kt:kt + 1]).then_inc(dv, 1)
                p.on("vector", corr)
                corr_tgts.append(p.inc(dv))
            corr_done = corr_tgts[-1]
            if False and dbg is not None and g == 0:
                p.wait("gpsimd", dv, corr_done)
                p.dma("gpsimd", cm.gq, dbg["e"][:, qb * 16384:(qb + 1) * 16384],
                      cm.pa[:, qb * 16384:(qb + 1) * 16384])
                if qb == 0:
                    p.dma("gpsimd", cm.gq, dbg["nm"][:, :], cm.nm[0][:, :])
                    p.dma("gpsimd", cm.gq, dbg["rs"][:, :], cm.rs[0][:, :])
                    p.dma("gpsimd", cm.gq, dbg["rden"][:, :], cm.rden[0][:, :])
                p.wait("gpsimd", cm.gq, p.ct[cm.gq])

            def ev_pt(gi, bank_ap, qb=qb):
                col = qb * 16384 + gi * 512
                p.on("vector", lambda e, col=col, b=bank_ap:
                     e.tensor_copy(cm.pa[:, col:col + 512], b).then_inc(dv, 1))
                return p.inc(dv)
            a5_last[g] = tr_batch(
                cm, [cm.pa[:, qb * 16384 + ch * 128: qb * 16384 + (ch + 1) * 128]
                     for ch in range(128)], ev_pt,
                pre_wait=lambda gi, ct=corr_tgts: ct[min(gi * 4 + 3, 31)])
        p.wait("tensor", dv, a5_last[g])
        p.wait("tensor", ac, evac2_last.get(g - 1, 0))
        vld = {}
        pv_done = {}
        for kt in range(32):
            vbuf = cm.pb[:, (kt % 2) * 8192:(kt % 2) * 8192 + 8192]
            if kt >= 2:
                p.wait("sync", pe, pv_done[kt - 2])
                p.wait("gpsimd", pe, pv_done[kt - 2])
            else:
                p.wait("sync", pe, smm_done[(31, GQ - 1)])
                p.wait("gpsimd", pe, smm_done[(31, GQ - 1)])
            in_v1 = bass.AP(vg_dram.tensor, kt * 512 * 2048,
                            [[2048, 128], [128 * 2048, 2], [1, 2048]])
            in_v2 = bass.AP(vg_dram.tensor, (kt * 512 + 256) * 2048,
                            [[2048, 128], [128 * 2048, 2], [1, 2048]])
            vsem = cm.sl[4 + kt % 2]
            vsem2 = cm.sl[6 + kt % 2]
            vld[kt] = p.dma("sync", vsem, vbuf[:, 0:4096], in_v1)
            vld2 = p.dma("gpsimd", vsem2, vbuf[:, 4096:8192], in_v2)
            p.wait("tensor", vsem, vld[kt])
            p.wait("tensor", vsem2, vld2)
            for qb in range(GQ):
                for kc in range(4):
                    col = qb * 16384 + kt * 512 + kc * 128
                    for dvt in range(4):
                        last = (kt == 31 and kc == 3)
                        p.mm(cm.pbk[qb * 4 + dvt][:, :],
                             cm.pa[:, col:col + 128],
                             vbuf[:, kc * 2048 + dvt * 512: kc * 2048 + (dvt + 1) * 512],
                             (kt == 0 and kc == 0), last,
                             sem=(pe if (qb == GQ - 1 and kc == 3 and dvt == 3) else None))
            pv_done[kt] = p.ct[pe]
        p.wait("vector", pe, pv_done[31])
        for qb in range(GQ):
            for dvt in range(4):
                p.chain("vector", dv, lambda e, qb=qb, dvt=dvt: e.tensor_scalar_mul(
                    cm.pb[:, 16384 + qb * 2048 + dvt * 512: 16384 + qb * 2048 + (dvt + 1) * 512],
                    cm.pbk[qb * 4 + dvt][:, :], cm.rden[qb][:, :]))
        bevac_last[g] = p.ct[dv]
        p.barrier(cm.bar)
        for qb in range(GQ):
            p.wait("tensor", dv, p.ct[dv])
            def ev_c(gi, bank_ap, qb=qb):
                out_ap = bass.AP(cm.pb, 20480 + gi * 4 * GQ * 128 + qb * 128,
                                 [[24576, 128], [GQ * 128, 4], [1, 128]])
                p.on("vector", lambda e, o=out_ap, b=bank_ap:
                     e.tensor_copy(o, b).then_inc(dv, 1))
                return p.inc(dv)
            tr_batch(cm, [cm.pb[:, 16384 + qb * 2048 + dvc * 128:
                                 16384 + qb * 2048 + (dvc + 1) * 128]
                          for dvc in range(16)], ev_c)
        fm_done = p.ct[dv]
        trc_pe[g] = p.ct[pe]
        p.wait("tensor", dv, fm_done)
        QW = GQ * 128
        evac2 = {}
        st_done = {}
        for c4 in range(4):
            in_ap = bass.AP(wo_dram.tensor, c4 * 512, [[2048, 128], [128 * 2048, 16], [1, 512]])
            wcol = cm.pb[:, (c4 % 2) * 8192:(c4 % 2) * 8192 + 8192]
            wsem = cm.sl[2 + c4 % 2]
            guard = womm.get((g, c4 - 2), pv_done[31])
            p.wait("sync", pe, guard)
            wt = p.dma("sync", wsem, wcol, in_ap)
            p.wait("tensor", wsem, wt)
            for c in range(4):
                cc = c4 * 4 + c
                bank = 6 + cc % 2
                if cc >= 2:
                    p.wait("tensor", ac, evac2[cc - 2])
                for ci in range(16):
                    p.mm(cm.pbk[bank][:, :QW],
                         wcol[:, ci * 512 + c * 128: ci * 512 + (c + 1) * 128],
                         cm.pb[:, 20480 + ci * QW: 20480 + (ci + 1) * QW],
                         ci == 0, False)
                pet = p.mm(cm.pbk[bank][:, :QW], cm.bias[bo_name][:1, cc * 128:(cc + 1) * 128],
                           cm.ones[:1, :QW], False, True, sem=pe)
                p.wait("scalar", pe, pet)
                p.wait("scalar", cm.ssm[cc % 2], p.ct[cm.ssm[cc % 2]])
                p.on("scalar", lambda e, cc=cc, bank=bank: e.copy(
                    cm.bst[cc % 2][:, :QW], cm.pbk[bank][:, :QW]).then_inc(ac, 1))
                evac2[cc] = p.inc(ac)
                p.wait("gpsimd", ac, evac2[cc])
                st_done[cc] = p.dma("gpsimd", cm.ssm[cc % 2],
                                    z_out_dram[cc * 128:(cc + 1) * 128, g * QW:(g + 1) * QW],
                                    cm.bst[cc % 2][:, :QW])
            womm[(g, c4)] = p.ct[pe]
        evac2_last[g] = evac2[15]
    cm.barrier()


def emit_z4_residual(cm, z4_dram, x1_src, x2_aps):
    p = cm.p
    ld, gq, pe, dv, ac = cm.ld, cm.gq, cm.pe, cm.dv, cm.ac
    for ci in range(16):
        p.dma("sync", cm.zl, cm.pa[:, ci * 2048:(ci + 1) * 2048],
              z4_dram[ci * 128:(ci + 1) * 128, :])
    p.wait("tensor", cm.zl, p.ct[cm.zl])
    st_done = [0, 0]
    xt_free = [0, 0]
    prev_tr = 0
    for t in range(16):
        fst = cm.fstage[t % 2]
        xsem = cm.sl[6 + t % 2]
        p.wait("sync", dv, xt_free[t % 2])
        x1t = p.dma("sync", xsem, cm.xt[t % 2][:, :], x1_src[t * 128:(t + 1) * 128, :])
        p.wait("vector", xsem, x1t)
        if t >= 2:
            p.wait("vector", cm.ssm[4 + t % 2], p.ct[cm.ssm[4 + t % 2]])
        p.wait("tensor", dv, prev_tr)

        def ev_z4(gi, bank_ap, fst=fst, t=t):
            p.on("vector", lambda e, gi=gi, b=bank_ap, fst=fst, t=t:
                 e.scalar_tensor_tensor(fst[:, gi * 512:(gi + 1) * 512], b, 1.0,
                                        cm.xt[t % 2][:, gi * 512:(gi + 1) * 512],
                                        op0=OP.mult, op1=OP.add).then_inc(dv, 1))
            return p.inc(dv)
        prev_tr = tr_batch(cm, [cm.pa[:, dc * 2048 + t * 128: dc * 2048 + (t + 1) * 128]
                                for dc in range(16)], ev_z4)
        xt_free[t % 2] = p.ct[dv]
        p.wait("gpsimd", dv, p.ct[dv])
        for ap_ in x2_aps:
            p.dma("gpsimd", cm.ssm[4 + t % 2], ap_[t * 128:(t + 1) * 128, :], fst[:, :])
    cm.barrier()


def emit_ffn(cm, z_dram, w1b_dram, w3b_dram, w2t_dram, xsrc, xsrc_is_f32, out_aps, dbg_hidden=None):
    p = cm.p
    ld, gq, pe, dv, ac = cm.ld, cm.gq, cm.pe, cm.dv, cm.ac
    st_done = {}
    up_done = {}
    dn_done = {}
    for tg in range(4):
        in_z = bass.AP(z_dram.tensor, tg * 512, [[2048, 128], [128 * 2048, 16], [1, 512]])
        zg = cm.pb[:, 0:8192]
        p.wait("sync", pe, up_done.get(tg - 1, 0))
        zt = p.dma("sync", cm.zl, zg, in_z)
        p.wait("tensor", cm.zl, zt)
        mm_done = {}
        sil_done = {}
        for hc2 in range(32):
            wb1 = cm.pb[:, 8192 + (hc2 % 2) * 4096: 8192 + (hc2 % 2) * 4096 + 4096]
            wb3 = cm.pb[:, 16384 + (hc2 % 2) * 4096: 16384 + (hc2 % 2) * 4096 + 4096]
            blk, off = (hc2 * 256) // 512, (hc2 * 256) % 512
            if hc2 >= 2:
                p.wait("sync", pe, mm_done[2 * (hc2 - 2) + 1])
            else:
                p.wait("sync", pe, dn_done.get(tg - 1, 0))
            in1 = bass.AP(w1b_dram.tensor, blk * 2048 * 512 + off, [[512, 128], [128 * 512, 16], [1, 256]])
            in3 = bass.AP(w3b_dram.tensor, blk * 2048 * 512 + off, [[512, 128], [128 * 512, 16], [1, 256]])
            s1m = cm.sl[0 + hc2 % 2]
            s3m = cm.sl[2 + hc2 % 2]
            t1 = p.dma("sync", s1m, wb1, in1)
            t3 = p.dma("sync", s3m, wb3, in3)
            p.wait("tensor", s1m, t1)
            p.wait("tensor", s3m, t3)
            for h in range(2):
                hc = hc2 * 2 + h
                bg, bl = (hc % 2) * 2, (hc % 2) * 2 + 1
                if hc >= 2:
                    p.wait("tensor", dv, sil_done[hc - 2])
                for ci in range(16):
                    p.mm(cm.pbk[bg][:, :], wb1[:, ci * 256 + h * 128: ci * 256 + (h + 1) * 128],
                         zg[:, ci * 512:(ci + 1) * 512], ci == 0, ci == 15,
                         sem=(pe if ci == 15 else None))
                for ci in range(16):
                    p.mm(cm.pbk[bl][:, :], wb3[:, ci * 256 + h * 128: ci * 256 + (h + 1) * 128],
                         zg[:, ci * 512:(ci + 1) * 512], ci == 0, ci == 15,
                         sem=(pe if ci == 15 else None))
                mm_done[hc] = p.ct[pe]
                p.wait("scalar", pe, mm_done[hc])
                if hc >= 2:
                    p.wait("scalar", dv, sil_done[hc - 2])
                p.on("scalar", lambda e, hc=hc, bg=bg: e.activation(
                    cm.bst[hc % 2][:, :512], cm.pbk[bg][:, :], AF.Silu).then_inc(ac, 1))
                sat = p.inc(ac)
                p.wait("vector", ac, sat)
                p.on("vector", lambda e, hc=hc, bl=bl: e.tensor_mul(
                    cm.pa[:, hc * 512:(hc + 1) * 512], cm.bst[hc % 2][:, :512],
                    cm.pbk[bl][:, :]).then_inc(dv, 1))
                sil_done[hc] = p.inc(dv)
        up_done[tg] = mm_done[63]
        p.wait("tensor", dv, sil_done[63])
        p.wait("sync", dv, sil_done[63])
        if dbg_hidden is not None and tg == 0:
            p.wait("gpsimd", dv, sil_done[63])
            p.dma("gpsimd", cm.gq, dbg_hidden[:, :], cm.pa[:, :])
        ev_done = 0
        w2mm = {}
        for dt in range(4):
            if dt > 0:
                p.wait("tensor", dv, ev_done)
            for h8 in range(8):
                w2buf = cm.pb[:, 8192 + (h8 % 2) * 4096: 8192 + (h8 % 2) * 4096 + 4096]
                w2s = cm.sl[4 + h8 % 2]
                guard = w2mm.get((dt, h8 - 2), w2mm.get((dt - 1, 6 + h8), 0))
                if guard:
                    p.wait("sync", pe, guard)
                in2 = bass.AP(w2t_dram.tensor, (h8 * 8 * 128) * 2048 + dt * 512,
                              [[2048, 128], [128 * 2048, 8], [1, 512]])
                wt = p.dma("sync", w2s, w2buf, in2)
                p.wait("tensor", w2s, wt)
                for hh in range(8):
                    hc = h8 * 8 + hh
                    for sub in range(4):
                        p.mm(cm.pbk[4 + sub][:, :],
                             cm.pa[:, hc * 512 + sub * 128: hc * 512 + (sub + 1) * 128],
                             w2buf[:, hh * 512:(hh + 1) * 512], hc == 0, hc == 63,
                             sem=(pe if sub == 3 else None))
                w2mm[(dt, h8)] = p.ct[pe]
            p.wait("vector", pe, w2mm[(dt, 7)])
            for sub in range(4):
                row0 = tg * 512 + sub * 128
                xr = cm.xres[sub % 2]
                xq = "sync" if xsrc_is_f32 else "gpsimd"
                xqs = cm.sl[6 + sub % 2]
                p.wait(xq, dv, st_done.get(("xr", sub % 2), 0))
                xt = p.dma(xq, xqs, xr[:, :], xsrc[row0:row0 + 128, dt * 512:(dt + 1) * 512])
                p.wait("vector", xqs, xt)
                fsem = cm.ssm[4 + sub % 2]
                if p.ct[fsem]:
                    p.wait("vector", fsem, p.ct[fsem])
                p.on("vector", lambda e, sub=sub, xr=xr: e.scalar_tensor_tensor(
                    cm.fstage[sub % 2][:, :512], cm.pbk[4 + sub][:, :], 1.0, xr[:, :],
                    op0=OP.mult, op1=OP.add).then_inc(dv, 1))
                dvt = p.inc(dv)
                st_done[("xr", sub % 2)] = dvt
                p.wait("gpsimd", dv, dvt)
                for ap_ in out_aps:
                    p.dma("gpsimd", fsem, ap_[row0:row0 + 128, dt * 512:(dt + 1) * 512],
                          cm.fstage[sub % 2][:, :512])
            ev_done = p.ct[dv]
        dn_done[tg] = p.ct[pe]
    cm.barrier()



def build_l1():
    nc = bass.Bass("TRN2", target_bir_lowering=False, detect_race_conditions=False)
    st = contextlib.ExitStack()
    x0 = nc.dram_tensor("x0", [T, D], BF16, kind="ExternalInput").ap()
    w1b = nc.dram_tensor("w1b", [16 * D, 512], BF16, kind="ExternalInput").ap()
    w3b = nc.dram_tensor("w3b", [16 * D, 512], BF16, kind="ExternalInput").ap()
    w2t = nc.dram_tensor("w2t", [HD, D], BF16, kind="ExternalInput").ap()
    wq = nc.dram_tensor("wq", [D, D], BF16, kind="ExternalInput").ap()
    wk = nc.dram_tensor("wk", [D, D], BF16, kind="ExternalInput").ap()
    wv = nc.dram_tensor("wv", [D, D], BF16, kind="ExternalInput").ap()
    bq = nc.dram_tensor("bq", [1, D], BF16, kind="ExternalInput").ap()
    bk = nc.dram_tensor("bk", [1, D], BF16, kind="ExternalInput").ap()
    bv = nc.dram_tensor("bv", [1, D], BF16, kind="ExternalInput").ap()
    idin = nc.dram_tensor("idin", [128, 128], BF16, kind="ExternalInput").ap()
    onein = nc.dram_tensor("onein", [1, 512], BF16, kind="ExternalInput").ap()
    x1 = nc.dram_tensor("x1", [T, D], F32, kind="ExternalOutput").ap()
    q = nc.dram_tensor("q", [D, T], BF16, kind="ExternalOutput").ap()
    k = nc.dram_tensor("k", [D, T], BF16, kind="ExternalOutput").ap()
    v = nc.dram_tensor("v", [T, D], BF16, kind="ExternalOutput").ap()
    zd = nc.dram_tensor("zd", [D, T], BF16, kind="Internal").ap()
    x1i = nc.dram_tensor("x1i", [T, D], F32, kind="Internal").ap()

    p = P(nc)
    cm = CM(nc, st, p)
    cm.load_consts(idin, onein, {"bq": bq, "bk": bk, "bv": bv})
    emit_ln(cm, x0, zd, src_is_f32=False)
    emit_ffn(cm, zd, w1b, w3b, w2t, x0, False, [x1, x1i])
    emit_ln(cm, x1i, zd, src_is_f32=True)
    emit_projs(cm, zd, [(wq, "bq", q, QSCALE, False), (wk, "bk", k, 1.0, False),
                        (wv, "bv", v, 1.0, True)])
    cm.final_waits()
    p.emit()
    st.close()
    return nc


def build_attn(last):
    nc = bass.Bass("TRN2", target_bir_lowering=False, detect_race_conditions=False)
    st = contextlib.ExitStack()
    qi = nc.dram_tensor("qi", [D, T], BF16, kind="ExternalInput").ap()
    kg = nc.dram_tensor("kg", [NTOK, D], BF16, kind="ExternalInput").ap()
    vg = nc.dram_tensor("vg", [NTOK, D], BF16, kind="ExternalInput").ap()
    wo = nc.dram_tensor("wo", [D, D], BF16, kind="ExternalInput").ap()
    bo = nc.dram_tensor("bo", [1, D], BF16, kind="ExternalInput").ap()
    idin = nc.dram_tensor("idin", [128, 128], BF16, kind="ExternalInput").ap()
    onein = nc.dram_tensor("onein", [1, 512], BF16, kind="ExternalInput").ap()
    zd = nc.dram_tensor("zd", [D, T], BF16, kind="Internal").ap()

    p = P(nc)
    cm = CM(nc, st, p)
    if not last:
        wq = nc.dram_tensor("wq", [D, D], BF16, kind="ExternalInput").ap()
        wk = nc.dram_tensor("wk", [D, D], BF16, kind="ExternalInput").ap()
        wv = nc.dram_tensor("wv", [D, D], BF16, kind="ExternalInput").ap()
        bq = nc.dram_tensor("bq", [1, D], BF16, kind="ExternalInput").ap()
        bk = nc.dram_tensor("bk", [1, D], BF16, kind="ExternalInput").ap()
        bv = nc.dram_tensor("bv", [1, D], BF16, kind="ExternalInput").ap()
        q = nc.dram_tensor("q", [D, T], BF16, kind="ExternalOutput").ap()
        k = nc.dram_tensor("k", [D, T], BF16, kind="ExternalOutput").ap()
        v = nc.dram_tensor("v", [T, D], BF16, kind="ExternalOutput").ap()
        cm.load_consts(idin, onein, {"bo": bo, "bq": bq, "bk": bk, "bv": bv})
        emit_attn_layer(cm, qi, kg, vg, wo, "bo", zd)
        emit_projs(cm, zd, [(wq, "bq", q, QSCALE, False), (wk, "bk", k, 1.0, False),
                            (wv, "bv", v, 1.0, True)])
    else:
        x1 = nc.dram_tensor("x1", [T, D], F32, kind="ExternalInput").ap()
        w1b = nc.dram_tensor("w1b", [16 * D, 512], BF16, kind="ExternalInput").ap()
        w3b = nc.dram_tensor("w3b", [16 * D, 512], BF16, kind="ExternalInput").ap()
        w2t = nc.dram_tensor("w2t", [HD, D], BF16, kind="ExternalInput").ap()
        y = nc.dram_tensor("y", [T, D], F32, kind="ExternalOutput").ap()
        x2i = nc.dram_tensor("x2i", [T, D], F32, kind="Internal").ap()
        cm.load_consts(idin, onein, {"bo": bo})
        emit_attn_layer(cm, qi, kg, vg, wo, "bo", zd)
        emit_z4_residual(cm, zd, x1, [x2i])
        emit_ln(cm, x2i, zd, src_is_f32=True)
        emit_ffn(cm, zd, w1b, w3b, w2t, x2i, True, [y])
    cm.final_waits()
    p.emit()
    st.close()
    return nc



_CACHE = {}


def _get(name, fn):
    if name not in _CACHE:
        _CACHE[name] = fn()
    return _CACHE[name]


def _run(nc, maps):
    return run_bass_kernel_spmd(nc, maps, core_ids=list(range(NC)))


def kernel(L0_attn_x, L0_attn_Wq, L0_attn_bq, L0_attn_Wk, L0_attn_bk,
           L0_attn_Wv, L0_attn_bv, L0_attn_Wo, L0_attn_bo,
           L0_ffn_W1, L0_ffn_W2, L0_ffn_W3):
    x = np.ascontiguousarray(np.asarray(L0_attn_x).reshape(NTOK, D))
    Wq = np.asarray(L0_attn_Wq)
    Wk = np.asarray(L0_attn_Wk)
    Wv = np.asarray(L0_attn_Wv)
    Wo = np.asarray(L0_attn_Wo)
    bq = np.asarray(L0_attn_bq)
    bk = np.asarray(L0_attn_bk)
    bv = np.asarray(L0_attn_bv)
    bo = np.asarray(L0_attn_bo)
    W1 = np.asarray(L0_ffn_W1)
    W2 = np.asarray(L0_ffn_W2)
    W3 = np.asarray(L0_ffn_W3)

    ident = np.ascontiguousarray(np.eye(128, dtype=np.float32).astype(bfdt))
    ones = np.ones((1, 512), dtype=bfdt)

    def blkT(W):
        Wt = np.ascontiguousarray(W.T)
        return np.ascontiguousarray(
            Wt.reshape(D, 16, 512).transpose(1, 0, 2).reshape(16 * D, 512))

    W1Tb = blkT(W1)
    W3Tb = blkT(W3)

    nc0 = _get("l0", build_l0)
    maps0 = []
    for c in range(NC):
        maps0.append({
            "w1s": np.ascontiguousarray(W1Tb[c * 2 * D:(c + 1) * 2 * D]),
            "w3s": np.ascontiguousarray(W3Tb[c * 2 * D:(c + 1) * 2 * D]),
            "w2s": np.ascontiguousarray(W2[c * 256:(c + 1) * 256]),
            "idin": ident,
        })
    r0 = _run(nc0, maps0)
    w1b = np.concatenate([r0.results[c]["w1b"] for c in range(NC)], axis=0)
    w3b = np.concatenate([r0.results[c]["w3b"] for c in range(NC)], axis=0)
    w2t = np.concatenate([r0.results[c]["w2t"] for c in range(NC)], axis=1)
    w1b = np.ascontiguousarray(w1b)
    w3b = np.ascontiguousarray(w3b)
    w2t = np.ascontiguousarray(w2t)

    nc1 = _get("l1", build_l1)
    maps1 = []
    for c in range(NC):
        maps1.append({
            "x0": np.ascontiguousarray(x[c * T:(c + 1) * T]),
            "w1b": w1b, "w3b": w3b, "w2t": w2t,
            "wq": np.ascontiguousarray(Wq[0]), "wk": np.ascontiguousarray(Wk[0]),
            "wv": np.ascontiguousarray(Wv[0]),
            "bq": np.ascontiguousarray(bq[0:1]), "bk": np.ascontiguousarray(bk[0:1]),
            "bv": np.ascontiguousarray(bv[0:1]),
            "idin": ident, "onein": ones,
        })
    r1 = _run(nc1, maps1)
    x1 = [r1.results[c]["x1"] for c in range(NC)]
    qs = [r1.results[c]["q"] for c in range(NC)]
    kgl = np.ascontiguousarray(np.concatenate([r1.results[c]["k"] for c in range(NC)], axis=0))
    vgl = np.ascontiguousarray(np.concatenate([r1.results[c]["v"] for c in range(NC)], axis=0))

    nca = _get("lattn", lambda: build_attn(False))
    for i in range(3):
        maps = []
        for c in range(NC):
            maps.append({
                "qi": np.ascontiguousarray(qs[c]), "kg": kgl, "vg": vgl,
                "wo": np.ascontiguousarray(Wo[i]), "bo": np.ascontiguousarray(bo[i:i + 1]),
                "wq": np.ascontiguousarray(Wq[i + 1]), "wk": np.ascontiguousarray(Wk[i + 1]),
                "wv": np.ascontiguousarray(Wv[i + 1]),
                "bq": np.ascontiguousarray(bq[i + 1:i + 2]),
                "bk": np.ascontiguousarray(bk[i + 1:i + 2]),
                "bv": np.ascontiguousarray(bv[i + 1:i + 2]),
                "idin": ident, "onein": ones,
            })
        r = _run(nca, maps)
        qs = [r.results[c]["q"] for c in range(NC)]
        kgl = np.ascontiguousarray(np.concatenate([r.results[c]["k"] for c in range(NC)], axis=0))
        vgl = np.ascontiguousarray(np.concatenate([r.results[c]["v"] for c in range(NC)], axis=0))

    ncl = _get("llast", lambda: build_attn(True))
    maps = []
    for c in range(NC):
        maps.append({
            "qi": np.ascontiguousarray(qs[c]), "kg": kgl, "vg": vgl,
            "wo": np.ascontiguousarray(Wo[3]), "bo": np.ascontiguousarray(bo[3:4]),
            "x1": np.ascontiguousarray(x1[c]),
            "w1b": w1b, "w3b": w3b, "w2t": w2t,
            "idin": ident, "onein": ones,
        })
    r = _run(ncl, maps)
    y = np.concatenate([r.results[c]["y"] for c in range(NC)], axis=0)
    return y.reshape(1, NTOK, D).astype(np.float32)
